# version 13
# baseline (speedup 1.0000x reference)
from contextlib import ExitStack

import numpy as np
import ml_dtypes

import concourse.bass as bass
import concourse.mybir as mybir
from concourse.bass_utils import run_bass_kernel_spmd

F32 = mybir.dt.float32
BF16 = mybir.dt.bfloat16
I32 = mybir.dt.int32
AF = mybir.ActivationFunctionType
ALU = mybir.AluOpType
AX = mybir.AxisListType

D = 1024
DH = 4096
NH = 16
HD = 64
T = 4096
TE = T + 2
P = 128
NCH = 8
HCH = 32
TILE = 512
NT_A = 8
WSTRIDE = 254
NWIN = 17
EPS_LN = 1e-5
EPS_AT = 1e-6


class Sem:
    __slots__ = ("h", "n")

    def __init__(self, es, nc, name):
        self.h = es.enter_context(nc.semaphore(name))
        self.n = 0


DBG = [False]
NORACE = [False]


def build_program():
    nc = bass.Bass("TRN2", target_bir_lowering=False, num_devices=8,
                   dynamic_dma_scratch_size=16384,
                   detect_race_conditions=not NORACE[0])
    ikind = "ExternalOutput" if DBG[0] else "Internal"

    xT = nc.dram_tensor("xT", [D, TE], F32, kind="ExternalInput").ap()
    xN = nc.dram_tensor("xN", [TE, D], F32, kind="ExternalInput").ap()
    wts = {}
    for nm, shp in (("wqT", [D, D]), ("wkT", [D, D]), ("wvT", [D, D]),
                    ("woT", [D, D]), ("w1T", [D, DH]), ("w2T", [DH, D])):
        wts[nm] = nc.dram_tensor(nm, shp, F32, kind="ExternalInput").ap()
    bias = {}
    for nm in ("bq", "bk", "bv", "bo"):
        bias[nm] = nc.dram_tensor(nm, [1, D], F32, kind="ExternalInput").ap()
    cw_d = nc.dram_tensor("cw", [P, HCH, 3], F32, kind="ExternalInput").ap()
    cb_d = nc.dram_tensor("cb", [P, HCH], F32, kind="ExternalInput").ap()
    msk_d = nc.dram_tensor("msk", [P, 2], F32, kind="ExternalInput").ap()
    out_d = nc.dram_tensor("out", [T, D], F32, kind="ExternalOutput").ap()

    k1_dram = nc.dram_tensor("k1_dram", [4352, D], BF16, kind=ikind).ap()
    x1_dram = nc.dram_tensor("x1_dram", [TE, D], F32, kind=ikind).ap()
    w1q_dram = nc.dram_tensor("w1q_dram", [D, DH], BF16, kind=ikind).ap()
    w2q_dram = nc.dram_tensor("w2q_dram", [DH, D], BF16, kind=ikind).ap()
    cc_in = nc.dram_tensor("cc_in", [P, 520], F32, kind="Internal").ap()
    dbg_d = nc.dram_tensor("dbg", [P, 26000], F32, kind="ExternalOutput").ap() if DBG[0] else None
    cc_out = nc.dram_tensor("cc_out", [P, 520], F32, kind="Internal").ap()

    consts_np = {
        "c_ones_col_b": np.ones((P, 1), ml_dtypes.bfloat16),
        "c_ones_col_f": np.ones((P, 1), np.float32),
        "c_ones_row_f": np.ones((1, P), np.float32),
        "c_one_f": np.ones((1, 1), np.float32),
        "c_ident_f": np.eye(P, dtype=np.float32),
    }
    e2 = np.zeros((2, P), ml_dtypes.bfloat16)
    e2[0, 0:64] = 1.0
    e2[1, 64:128] = 1.0
    consts_np["c_exp2"] = e2
    consts_dram = {k: nc.inline_tensor(v, k).ap() for k, v in consts_np.items()}

    es = ExitStack()

    def sball(stk, name, shape, dt):
        return stk.enter_context(nc.sbuf_tensor(name, shape, dt))

    ones_col_b = sball(es, "ones_col_b", [P, 1], BF16)
    ones_col_f = sball(es, "ones_col_f", [P, 1], F32)
    ones_row_f = sball(es, "ones_row_f", [1, P], F32)
    one_f = sball(es, "one_f", [1, 1], F32)
    ident_f = sball(es, "ident_f", [P, P], F32)
    exp2_b = sball(es, "exp2_b", [2, P], BF16)
    crow = sball(es, "crow", [1, 32], F32)

    spacer = sball(es, "spacer", [P, 128], F32)
    s1i_col = sball(es, "s1i_col", [P, 34], F32)
    msk_t = sball(es, "msk_t", [P, 2], F32)
    cwt = sball(es, "cwt", [P, HCH, 3], F32)
    cbt = sball(es, "cbt", [P, HCH], F32)
    cwa = sball(es, "cwa", [P, HCH], F32)
    cwb = sball(es, "cwb", [P, HCH], F32)
    mw2c = sball(es, "mw2c", [P, 1], F32)
    mw2c127 = sball(es, "mw2c127", [P, 1], F32)
    eps15_col = sball(es, "eps15_col", [P, 1], F32)
    c127_col = sball(es, "c127_col", [P, 1], F32)
    swc = sball(es, "swc", [P, 2], F32)

    s1r = sball(es, "s1r", [1, 3 * P + 256 + 16], F32)
    s2r = sball(es, "s2r", [1, 256], F32)
    s2cc = sball(es, "s2cc", [P, 48], F32)
    rcol = sball(es, "rcol", [P, 8], F32)

    ab1 = ExitStack()
    r_row = sball(ab1, "r_row", [1, 4104], F32)
    absb = sball(ab1, "absb", [P, 256], F32)
    crow2 = sball(ab1, "crow2", [1, 32], F32)
    nm_row = sball(ab1, "nm_row", [1, 4104], BF16)
    wq_b = sball(ab1, "wq_b", [P, NCH, D], BF16)
    wsrow_b = sball(ab1, "wsrow_b", [1, 3 * D], BF16)
    kvz_sb = sball(ab1, "kvz_sb", [P, 520], F32)
    kvp_b = sball(ab1, "kvp_b", [P, NH * HD], BF16)
    zp_b = sball(ab1, "zp_b", [P, NH], BF16)
    srow = sball(ab1, "srow", [1, 2080], F32)

    sA_ = ExitStack()
    wk_b = sball(sA_, "wk_b", [P, NCH, D], BF16)
    wv_b = sball(sA_, "wv_b", [P, NCH, D], BF16)
    bkrep = sball(sA_, "bkrep", [P, D], F32)
    bvrep = sball(sA_, "bvrep", [P, D], F32)
    xa = sball(sA_, "xa", [P, 2, NCH, TILE], BF16)
    xsq = sball(sA_, "xsq", [P, NCH, TILE], BF16)
    k_tm = sball(sA_, "k_tm", [P, 2, D], BF16)
    v_tm = sball(sA_, "v_tm", [P, 2, NH, HD + 1], BF16)
    kvt1 = sball(sA_, "kvt1", [P, 2, D], F32)
    vvt1 = sball(sA_, "vvt1", [P, 2, D], F32)
    ke = sball(sA_, "ke", [P, 2, D], BF16)

    sW_ = ExitStack()
    DH2 = DH // 2
    wst = sball(sW_, "wst", [P, 2, DH2], F32)
    wqo = sball(sW_, "wqo", [P, DH2], BF16)
    wqi = sball(sW_, "wqi", [P, DH2], I32)

    sems = {}
    semstk = ExitStack()

    def S_(name):
        s = Sem(semstk, nc, name)
        sems[name] = s
        return s

    ld_c = S_("ld_c"); ld_w = S_("ld_w"); ld_xa = S_("ld_xa")
    ld_xb = S_("ld_xb"); ld_xt = S_("ld_xt"); ld_k1 = S_("ld_k1")
    ld_x1 = S_("ld_x1"); sp_k1 = S_("sp_k1"); sp_x1 = S_("sp_x1")
    st_o = S_("st_o"); xc_s = S_("xc_s"); cc_s = S_("cc_s")
    ld_wq = S_("ld_wq"); sp_wq = S_("sp_wq"); dbg_s = S_("dbg_s")
    tS = S_("tS"); vS = S_("vS"); sS = S_("sS"); gS = S_("gS")

    prog = {"sync": [], "scalar": [], "tensor": [], "vector": [], "gpsimd": []}

    def E(eng, f, sem=None, k=1):
        if sem is None:
            prog[eng].append(f)
            return None
        prog[eng].append(lambda e, f=f, s=sem.h, k=k: f(e).then_inc(s, k))
        sem.n += k
        return sem.n

    def SPC():
        E("vector", lambda e: e.memset(spacer[:, 0:64], 0.0), vS)

    def SPC4():
        E("vector", lambda e: e.memset(spacer[:, 0:64], 0.0), vS)
        E("vector", lambda e: e.tensor_copy(spacer[:, 64:128], spacer[:, 0:64]), vS)
        E("vector", lambda e: e.tensor_copy(spacer[:, 0:64], spacer[:, 64:128]), vS)
        E("vector", lambda e: e.tensor_copy(spacer[:, 64:128], spacer[:, 0:64]), vS)
        E("vector", lambda e: e.tensor_copy(spacer[:, 0:64], spacer[:, 64:128]), vS)

    def W(eng, sem, val):
        if val is None or val <= 0:
            return
        prog[eng].append(lambda e, s=sem.h, v=val: e.wait_ge(s, v))

    for k in consts_np:
        dst = {"c_ones_col_b": ones_col_b, "c_ones_col_f": ones_col_f,
               "c_ones_row_f": ones_row_f,
               "c_one_f": one_f, "c_ident_f": ident_f, "c_exp2": exp2_b}[k]
        E("sync", lambda e, d=dst, s=consts_dram[k]: e.dma_start(out=d[:], in_=s), ld_c, 16)
    E("sync", lambda e: e.dma_start(out=cwt[:], in_=cw_d), ld_c, 16)
    E("sync", lambda e: e.dma_start(out=cbt[:], in_=cb_d), ld_c, 16)
    LD_C = E("sync", lambda e: e.dma_start(out=msk_t[:], in_=msk_d), ld_c, 16)
    brow_q = srow

    E("gpsimd", lambda e: e.dma_start(out=wk_b[:], in_=wts["wkT"].rearrange("(c p) n -> p c n", p=P)), ld_w, 16)
    LDW_KV = E("gpsimd", lambda e: e.dma_start(out=wv_b[:], in_=wts["wvT"].rearrange("(c p) n -> p c n", p=P)), ld_w, 16)
    LDW_Q = E("gpsimd", lambda e: e.dma_start(out=wq_b[:], in_=wts["wqT"].rearrange("(c p) n -> p c n", p=P)), ld_w, 16)

    psA = ExitStack()
    kps = [psA.enter_context(nc.psum_tensor("kps0", [P, 512], F32)),
           psA.enter_context(nc.psum_tensor("kps1", [P, 512], F32))]
    vps = [psA.enter_context(nc.psum_tensor("vps0", [P, 512], F32)),
           psA.enter_context(nc.psum_tensor("vps1", [P, 512], F32))]
    kvzp = [psA.enter_context(nc.psum_tensor("kvzp0", [P, 512], F32)),
            psA.enter_context(nc.psum_tensor("kvzp1", [P, 512], F32))]
    stp = psA.enter_context(nc.psum_tensor("stp", [1, 512], F32))
    rcp = psA.enter_context(nc.psum_tensor("rcp", [P, 512], F32))

    stp_free = [0]
    rcp_free = [0]

    def wsum(w_b, slot, ldneed):
        W("tensor", ld_w, ldneed)
        vdone = None
        for h in range(2):
            W("tensor", vS, stp_free[0])
            tdone = None
            for c in range(NCH):
                tdone = E("tensor", lambda e, c=c, h=h, w=w_b: e.matmul(
                    stp[0:1, 0:512], ones_col_b[:],
                    w[:, c, h * 512:(h + 1) * 512],
                    start=(c == 0), stop=(c == NCH - 1)),
                    tS if c == NCH - 1 else None)
            W("vector", tS, tdone)
            vdone = E("vector", lambda e, s=slot, h=h: e.tensor_copy(
                wsrow_b[0:1, s * D + h * 512:s * D + (h + 1) * 512],
                stp[0:1, 0:512]), vS)
            stp_free[0] = vdone
        return vdone

    W("tensor", ld_c, LD_C)
    WK_SUM = wsum(wk_b, 0, LDW_KV)
    WV_SUM = wsum(wv_b, 1, LDW_KV)
    WQ_SUM = wsum(wq_b, 2, LDW_Q)

    REPS_V = 0
    for nm, dst in (("bk", bkrep), ("bv", bvrep)):
        ldb = E("sync", lambda e, nm=nm: e.dma_start(out=srow[0:1, 1024:2048], in_=bias[nm]), ld_c, 16)
        W("tensor", ld_c, ldb)
        tdone = None
        for h in range(2):
            tdone = E("tensor", lambda e, h=h: e.matmul(
                kps[h][:, 0:512], ones_row_f[:],
                srow[0:1, 1024 + h * 512:1024 + (h + 1) * 512],
                start=True, stop=True), tS)
        W("vector", tS, tdone)
        for h in range(2):
            REPS_V = E("vector", lambda e, d=dst, h=h: e.tensor_copy(
                d[:, h * 512:(h + 1) * 512], kps[h][:, 0:512]), vS)
        W("sync", vS, REPS_V)

    W("vector", ld_c, LD_C)
    E("vector", lambda e: e.reciprocal(cwa[:], cwt[:, :, 1]), vS)
    E("vector", lambda e: e.memset(kvz_sb[:], 0.0), vS)
    E("vector", lambda e: e.tensor_mul(cwb[:], cwa[:], cwt[:, :, 2]), vS)
    E("vector", lambda e: e.memset(absb[:], 0.0), vS)
    E("vector", lambda e: e.tensor_mul(cwa[:], cwa[:], cwt[:, :, 0]), vS)
    E("vector", lambda e: e.memset(s1i_col[:], 0.0), vS)
    E("vector", lambda e: e.memset(crow[:], 0.0), vS)
    E("vector", lambda e: e.memset(s2cc[:], 0.0), vS)
    E("vector", lambda e: e.memset(eps15_col[:], 1e-5), vS)
    E("vector", lambda e: e.memset(c127_col[:], 1.0 / 127.0), vS)
    PRESET_V = E("vector", lambda e: e.memset(v_tm[:, :, :, HD:HD + 1], 1.0), vS)

    wq_steps = []

    def emit_wq_load(step, par):
        if step < 16:
            c, hh = step // 2, step % 2
            src = wts["w1T"].rearrange("(c p) n -> p c n", p=P)[:, c, hh * DH2:(hh + 1) * DH2]
        else:
            s2 = step - 16
            src = wts["w2T"].rearrange("(c p) n -> p c n", p=P)[:, 2 * s2:2 * (s2 + 1), :]
            return E("gpsimd", lambda e, src=src, par=par: e.dma_start(
                out=wst[:, par, :].rearrange("p (c n) -> p c n", n=D), in_=src), ld_wq, 16)
        return E("gpsimd", lambda e, src=src, par=par: e.dma_start(
            out=wst[:, par, :], in_=src), ld_wq, 16)

    def emit_wq_pass1(step, par, ldval):
        W("vector", ld_wq, ldval)
        return E("vector", lambda e, step=step, par=par: e.tensor_reduce(
            absb[:, 8 * step:8 * step + 1], wst[:, par, :], AX.X, ALU.add,
            apply_absolute_value=True), vS)

    def emit_wq_scales(v_abs):
        W("tensor", vS, max(v_abs, stp_free[0]))
        t_sum = E("tensor", lambda e: e.matmul(
            stp[0:1, 0:256], ones_col_f[:], absb[:, 0:256], start=True, stop=True), tS)
        W("vector", tS, t_sum)
        crow_v = crow[:].rearrange("a (b c) -> a b c", c=8)
        E("vector", lambda e: e.tensor_reduce(crow[0:1, 0:1], stp[0:1, 0:128], AX.X, ALU.add), vS)
        sfree = E("vector", lambda e: e.tensor_reduce(crow[0:1, 8:9], stp[0:1, 128:256], AX.X, ALU.add), vS)
        stp_free[0] = sfree
        SPC4()
        E("vector", lambda e: e.tensor_scalar(crow_v[0:1, 0:2, 0:1], crow_v[0:1, 0:2, 0:1], 1.0 / (D * DH), None, ALU.mult), vS)
        SPC4()
        E("vector", lambda e: e.tensor_scalar(crow_v[0:1, 2:4, 0:1], crow_v[0:1, 0:2, 0:1], eps15_col[0:1], None, ALU.max), vS)
        SPC4()
        E("vector", lambda e: e.reciprocal(crow_v[0:1, 2:4, 0:1], crow_v[0:1, 2:4, 0:1]), vS)
        SPC4()
        v_sw = E("vector", lambda e: e.tensor_copy(crow2[0:1, :], crow[0:1, :]), vS)
        W("tensor", vS, max(v_sw, rcp_free[0]))
        t_bc = None
        for i in range(3):
            src = [crow2[0:1, 16:17], crow2[0:1, 24:25], crow2[0:1, 8:9]][i]
            dst = [rcp[:, 0:1], rcp[:, 1:2], rcp[:, 2:3]][i]
            t_bc = E("tensor", lambda e, s=src, d=dst: e.matmul(
                d, ones_row_f[:], s, start=True, stop=True), tS)
        W("vector", tS, t_bc)
        E("vector", lambda e: e.tensor_copy(swc[:, 0:2], rcp[:, 0:2]), vS)
        E("vector", lambda e: e.tensor_copy(mw2c[:], rcp[:, 2:3]), vS)
        vdone = E("vector", lambda e: e.tensor_scalar(
            mw2c127[:], rcp[:, 2:3], 1.0 / 127.0, None, ALU.mult), vS)
        rcp_free[0] = vdone
        return vdone

    def emit_wq_pass2(step, par, ldval, scales_done, prev_spill):
        W("vector", ld_wq, ldval)
        W("vector", vS, scales_done)
        sl = 0 if step < 16 else 1
        E("vector", lambda e, par=par, sl=sl: e.tensor_scalar(
            wqi[:], wst[:, par, :], swc[:, sl:sl + 1], None, ALU.mult), vS)
        W("vector", sp_wq, prev_spill)
        v_q = E("vector", lambda e: e.tensor_scalar(
            wqo[:], wqi[:], 1.0, -1.0, ALU.min, ALU.max), vS)
        W("sync", vS, v_q)
        if step < 16:
            c, hh = step // 2, step % 2
            dst = w1q_dram.rearrange("(c p) n -> p c n", p=P)[:, c, hh * DH2:(hh + 1) * DH2]
            spdone = E("sync", lambda e, dst=dst: e.dma_start(out=dst, in_=wqo[:]), sp_wq, 16)
        else:
            s2 = step - 16
            dst = w2q_dram.rearrange("(c p) n -> p c n", p=P)[:, 2 * s2:2 * (s2 + 1), :]
            spdone = E("sync", lambda e, dst=dst: e.dma_start(
                out=dst, in_=wqo[:].rearrange("p (c n) -> p c n", n=D)), sp_wq, 16)
        return spdone

    xa_free = [0, 0]
    kvz_acc_v = [0]
    prev_tstats = [0]
    kvz_t = {}
    kvt1_free = [0, 0]
    vvt1_free = [0, 0]
    rcol_free = [0, 0, 0, 0]
    pend = []
    KVZ_LAST = [0]
    kps_free = [0]
    vps_free = [0]
    v_rows_at = {}
    wq_p1 = [0, 0]
    wq_state = {"loads": [], "p1done": [], "scales": None, "p2next": 0}

    def kvz_mms(gb, par, vneed, gneed):
        W("tensor", vS, vneed)
        W("tensor", vS, kvz_acc_v[0])
        W("tensor", gS, gneed)
        r = None
        for p_ in range(8):
            bank = p_ // 4
            off = (p_ % 4) * 65
            for hh in range(2):
                h = 2 * p_ + hh
                r = E("tensor", lambda e, par=par, h=h, hh=hh, bank=bank, off=off: e.matmul(
                          kvzp[bank][hh * 64:(hh + 1) * 64, off:off + 65],
                          k_tm[:, par, h * HD:(h + 1) * HD],
                          v_tm[:, par, h, :],
                          start=True, stop=True),
                      tS if (p_ == 7 and hh == 1) else None)
        kvz_t[gb] = r
        W("vector", tS, r)
        for bank in range(2):
            kvz_acc_v[0] = E("vector", lambda e, bank=bank: e.tensor_add(
                kvz_sb[:, bank * 260:(bank + 1) * 260],
                kvz_sb[:, bank * 260:(bank + 1) * 260],
                kvzp[bank][:, 0:260]), vS)
        if gb == NT_A * 4 - 1:
            KVZ_LAST[0] = kvz_acc_v[0]

    for it in range(NT_A):
        par = it % 2
        base = 1 + it * TILE
        if it >= 2:
            W("gpsimd", tS, xa_free[par])
        lda = E("gpsimd", lambda e, it=it, par=par: e.dma_start(
            out=xa[:, par], in_=xT[:, 1 + it * TILE: 1 + (it + 1) * TILE]
            .rearrange("(c p) n -> p c n", p=P)), ld_xa, 16)

        for sub in range(4):
            step = it * 4 + sub
            wpar = step % 2
            if it < 8:
                if step < 32:
                    if step >= 2:
                        W("gpsimd", vS, wq_p1[wpar])
                    lv = emit_wq_load(step, wpar)
                    wq_state["loads"].append(lv)
                    v1 = emit_wq_pass1(step, wpar, lv)
                    wq_p1[wpar] = v1
                    wq_state["p1done"].append(v1)

        W("vector", ld_xa, lda)
        W("vector", tS, prev_tstats[0])
        v_sq = E("vector", lambda e, par=par: e.tensor_mul(xsq[:], xa[:, par], xa[:, par]), vS)

        W("tensor", ld_xa, lda)
        if it == 0:
            W("tensor", ld_w, LDW_KV)
            W("tensor", vS, max(WK_SUM, WV_SUM, REPS_V, PRESET_V))
        W("tensor", vS, max(v_sq, stp_free[0]))
        t_s1 = None
        for c in range(NCH):
            t_s1 = E("tensor", lambda e, c=c, par=par: e.matmul(
                stp[0:1, 0:512], ones_col_b[:], xa[:, par, c, :],
                start=(c == 0), stop=(c == NCH - 1)),
                tS if c == NCH - 1 else None)
        W("vector", tS, t_s1)
        v_m = E("vector", lambda e: e.tensor_scalar(srow[0:1, 0:512], stp[0:1, 0:512], 1.0 / D, None, ALU.mult), vS)
        W("tensor", vS, v_m)
        t_stats = None
        for c in range(NCH):
            t_stats = E("tensor", lambda e, c=c: e.matmul(
                stp[0:1, 0:512], ones_col_b[:], xsq[:, c, :],
                start=(c == 0), stop=(c == NCH - 1)),
                tS if c == NCH - 1 else None)
        prev_tstats[0] = t_stats
        W("vector", tS, t_stats)
        v_stfree = E("vector", lambda e: e.tensor_scalar(srow[0:1, 512:1024], stp[0:1, 0:512], 1.0 / D, None, ALU.mult), vS)
        stp_free[0] = v_stfree
        E("vector", lambda e: e.tensor_mul(srow[0:1, 1024:1536], srow[0:1, 0:512], srow[0:1, 0:512]), vS)
        v_var = E("vector", lambda e: e.scalar_tensor_tensor(
            srow[0:1, 512:1024], srow[0:1, 512:1024], EPS_LN, srow[0:1, 1024:1536],
            ALU.add, ALU.subtract), vS)
        W("scalar", vS, v_var)
        s_sq = E("scalar", lambda e: e.activation(out=srow[0:1, 1024:1536], in_=srow[0:1, 512:1024], func=AF.Sqrt), sS)
        W("vector", sS, s_sq)
        E("vector", lambda e, base=base: e.reciprocal(r_row[0:1, base:base + TILE], srow[0:1, 1024:1536]), vS)
        v_rows = E("vector", lambda e, base=base: e.tensor_scalar(
            nm_row[0:1, base:base + TILE], srow[0:1, 0:512], -1.0, None, ALU.mult), vS)

        for blk in range(4):
            gb = it * 4 + blk
            bpar = gb % 2
            if pend and pend[0][0] == gb - 2:
                pg = pend.pop(0)
                kvz_mms(pg[0], pg[1], pg[2], pg[3])
            W("tensor", sS, kps_free[0])
            for c in range(NCH):
                for h in range(2):
                    E("tensor", lambda e, c=c, h=h, par=par, blk=blk: e.matmul(
                        kps[h][:, 0:512], xa[:, par, c, blk * P:(blk + 1) * P],
                        wk_b[:, c, h * 512:(h + 1) * 512],
                        start=(c == 0), stop=False))
            if blk == 0:
                W("tensor", vS, v_rows)
            t_k = None
            for h in range(2):
                t_k = E("tensor", lambda e, h=h, base=base, blk=blk: e.matmul(
                    kps[h][:, 0:512],
                    nm_row[0:1, base + blk * P: base + (blk + 1) * P],
                    wsrow_b[0:1, h * 512:(h + 1) * 512],
                    start=False, stop=True), tS)
            W("tensor", sS, vps_free[0])
            t_v = None
            for c in range(NCH):
                for h in range(2):
                    E("tensor", lambda e, c=c, h=h, par=par, blk=blk: e.matmul(
                        vps[h][:, 0:512], xa[:, par, c, blk * P:(blk + 1) * P],
                        wv_b[:, c, h * 512:(h + 1) * 512],
                        start=(c == 0), stop=False))
            for h in range(2):
                t_v = E("tensor", lambda e, h=h, base=base, blk=blk: e.matmul(
                    vps[h][:, 0:512],
                    nm_row[0:1, base + blk * P: base + (blk + 1) * P],
                    wsrow_b[0:1, D + h * 512: D + (h + 1) * 512],
                    start=False, stop=True), tS)
            if blk == 3:
                xa_free[par] = tS.n
            W("tensor", vS, rcp_free[0])
            t_rc = E("tensor", lambda e, base=base, blk=blk: e.matmul(
                rcp[:, blk:blk + 1],
                r_row[0:1, base + blk * P: base + (blk + 1) * P], one_f[:],
                start=True, stop=True), tS)

            W("vector", tS, t_rc)
            W("vector", sS, rcol_free[blk])
            v_rc = E("vector", lambda e, blk=blk: e.tensor_copy(
                rcol[:, blk:blk + 1], rcp[:, blk:blk + 1]), vS)
            rcp_free[0] = v_rc
            W("scalar", tS, t_k)
            W("scalar", vS, max(v_rc, kvt1_free[bpar]))
            s_k1 = None
            for h in range(2):
                s_k1 = E("scalar", lambda e, h=h, bpar=bpar, blk=blk: e.activation(
                    out=kvt1[:, bpar, h * 512:(h + 1) * 512], in_=kps[h][:, 0:512],
                    func=AF.Copy, bias=0.0, scale=rcol[:, blk:blk + 1]), sS)
            kps_free[0] = s_k1
            W("scalar", tS, t_v)
            W("scalar", gS, vvt1_free[bpar])
            s_v1 = None
            for h in range(2):
                s_v1 = E("scalar", lambda e, h=h, bpar=bpar, blk=blk: e.activation(
                    out=vvt1[:, bpar, h * 512:(h + 1) * 512], in_=vps[h][:, 0:512],
                    func=AF.Copy, bias=0.0, scale=rcol[:, blk:blk + 1]), sS)
            vps_free[0] = s_v1
            W("vector", sS, s_k1)
            v_kz = E("vector", lambda e, bpar=bpar: e.tensor_add(
                kvt1[:, bpar, :], kvt1[:, bpar, :], bkrep[:]), vS)
            W("scalar", vS, v_kz)
            s_ke = E("scalar", lambda e, bpar=bpar: e.activation(
                out=ke[:, bpar, :], in_=kvt1[:, bpar, :], func=AF.Exp), sS)
            W("gpsimd", vS, v_kz)
            W("gpsimd", sS, s_ke)
            g_ku = E("gpsimd", lambda e, bpar=bpar: e.tensor_scalar(
                kvt1[:, bpar, :], kvt1[:, bpar, :], 0.0, None, ALU.max), gS)
            W("vector", sS, s_ke)
            W("vector", gS, g_ku)
            W("vector", tS, kvz_t.get(gb - 2, 0))
            v_k = E("vector", lambda e, bpar=bpar: e.scalar_tensor_tensor(
                k_tm[:, bpar, :], ke[:, bpar, :], 1.0, kvt1[:, bpar, :],
                ALU.min, ALU.add), vS)
            W("gpsimd", sS, s_v1)
            W("gpsimd", tS, kvz_t.get(gb - 2, 0))
            g_v = E("gpsimd", lambda e, bpar=bpar: e.tensor_add(
                v_tm[:, bpar, :, 0:HD],
                vvt1[:, bpar].rearrange("p (h c) -> p h c", c=HD),
                bvrep[:].rearrange("p (h c) -> p h c", c=HD)), gS)
            kvt1_free[bpar] = v_k
            vvt1_free[bpar] = g_v
            rcol_free[blk] = s_v1
            pend.append((gb, bpar, v_k, g_v))

    while pend:
        pg = pend.pop(0)
        kvz_mms(pg[0], pg[1], pg[2], pg[3])

    if DBG[0]:
        W("sync", vS, wq_state["p1done"][-1])
        E("sync", lambda e: e.dma_start(out=dbg_d[:, 25000:25256], in_=absb[:, 0:256]), st_o, 16)
    SCALES_V = emit_wq_scales(wq_state["p1done"][-1])
    if DBG[0]:
        W("sync", vS, SCALES_V)
        E("sync", lambda e: e.dma_start(out=dbg_d[4:5, 25700:25732], in_=crow[0:1, :]), st_o, 16)
        W("sync", vS, vS.n)
        E("sync", lambda e: e.dma_start(out=dbg_d[4:5, 1:4097], in_=r_row[0:1, 1:4097]), st_o, 16)
    wq_p2 = [wq_p1[0], wq_p1[1]]
    SP_WQ_ALL = 0
    for step in range(32):
        wpar = step % 2
        W("gpsimd", vS, wq_p2[wpar])
        lv = emit_wq_load(step, wpar)
        sp = emit_wq_pass2(step, wpar, lv,
                           SCALES_V if step == 0 else 0, SP_WQ_ALL)
        wq_p2[wpar] = vS.n
        SP_WQ_ALL = sp

    v_kvz = KVZ_LAST[0]
    W("sync", vS, v_kvz)
    E("sync", lambda e: e.dma_start(out=cc_in, in_=kvz_sb[:]), xc_s, 16)
    W("gpsimd", xc_s, xc_s.n)
    E("gpsimd", lambda e: e.collective_compute(
        "AllReduce", ALU.add, replica_groups=[[0, 1, 2, 3], [4, 5, 6, 7]],
        ins=[cc_in], outs=[cc_out]), cc_s, 1)
    W("sync", cc_s, cc_s.n)
    XC_D = E("sync", lambda e: e.dma_start(out=kvz_sb[:], in_=cc_out), xc_s, 16)
    if DBG[0]:
        W("sync", xc_s, XC_D)
        E("sync", lambda e: e.dma_start(out=dbg_d[:, 25100:25620], in_=kvz_sb[:]), st_o, 16)
    W("vector", xc_s, XC_D)
    E("vector", lambda e: e.memset(kvp_b[:], 0.0), vS)
    E("vector", lambda e: e.memset(zp_b[:], 0.0), vS)
    KVZ_RDY = 0
    for p_ in range(8):
        E("vector", lambda e, p_=p_: e.tensor_copy(
            kvp_b[0:64, 2 * p_ * HD:(2 * p_ + 1) * HD],
            kvz_sb[0:64, p_ * 65:p_ * 65 + 64]), vS)
        E("vector", lambda e, p_=p_: e.tensor_copy(
            kvp_b[64:128, (2 * p_ + 1) * HD:(2 * p_ + 2) * HD],
            kvz_sb[64:128, p_ * 65:p_ * 65 + 64]), vS)
        E("vector", lambda e, p_=p_: e.tensor_copy(
            zp_b[0:64, 2 * p_:2 * p_ + 1], kvz_sb[0:64, p_ * 65 + 64:p_ * 65 + 65]), vS)
        KVZ_RDY = E("vector", lambda e, p_=p_: e.tensor_copy(
            zp_b[64:128, 2 * p_ + 1:2 * p_ + 2], kvz_sb[64:128, p_ * 65 + 64:p_ * 65 + 65]), vS)

    psA.close()

    A_T = tS.n
    A_V = vS.n
    A_S = sS.n
    A_G = gS.n

    sW_.close()
    sA_.close()

    b1 = ExitStack()
    wo_b = sball(b1, "wo_b", [P, NCH, D], BF16)
    borep = sball(b1, "borep", [P, D], F32)
    bqcol = sball(b1, "bqcol", [P, NCH], F32)
    xb1 = sball(b1, "xb1", [P, 2, NCH, TILE], BF16)
    xtm = sball(b1, "xtm", [P, 2, 4, D], F32)
    q_bf = sball(b1, "q_bf", [P, NCH, TILE], BF16)
    qt2 = sball(b1, "qt2", [P, 2, TILE], F32)
    qe = sball(b1, "qe", [P, TILE], BF16)
    qu = sball(b1, "qu", [P, TILE], F32)
    rd_f = sball(b1, "rd_f", [2, TILE], F32)
    rd_b = sball(b1, "rd_b", [2, TILE], BF16)
    rdsb = sball(b1, "rdsb", [P, TILE], F32)
    rbsb = sball(b1, "rbsb", [P, TILE], F32)
    attn_b = sball(b1, "attn_b", [P, NCH, TILE], BF16)
    x1t = sball(b1, "x1t", [P, 4, D], F32)
    x1s = sball(b1, "x1s", [P, D], F32)
    xn2 = sball(b1, "xn2", [P, D], F32)
    k1i = sball(b1, "k1i", [P, D], I32)
    k1b = sball(b1, "k1b", [P, 4, D], BF16)
    scc = sball(b1, "scc", [P, 8], F32)

    psB = ExitStack()
    qps = [psB.enter_context(nc.psum_tensor("qps0", [P, 512], F32)),
           psB.enter_context(nc.psum_tensor("qps1", [P, 512], F32))]
    rbc = psB.enter_context(nc.psum_tensor("rbc", [P, 512], F32))
    prp = psB.enter_context(nc.psum_tensor("prp", [P, 512], F32))
    dnp = psB.enter_context(nc.psum_tensor("dnp", [2, 512], F32))
    rdp = psB.enter_context(nc.psum_tensor("rdp", [P, 512], F32))
    ops_ = [psB.enter_context(nc.psum_tensor("ops0", [P, 512], F32)),
            psB.enter_context(nc.psum_tensor("ops1", [P, 512], F32))]

    def fence(eng):
        W(eng, tS, A_T)
        W(eng, vS, A_V)
        W(eng, sS, A_S)
        W(eng, gS, A_G)
        W(eng, sp_wq, SP_WQ_ALL)

    fence("gpsimd")
    fence("sync")

    LDW_O = E("gpsimd", lambda e: e.dma_start(out=wo_b[:], in_=wts["woT"].rearrange("(c p) n -> p c n", p=P)), ld_w, 16)
    ldb = E("sync", lambda e: e.dma_start(out=srow[0:1, 1024:2048], in_=bias["bo"]), ld_c, 16)
    W("tensor", ld_c, ldb)
    W("tensor", vS, A_V)
    psB_first = None
    tdone = None
    for h in range(2):
        tdone = E("tensor", lambda e, h=h: e.matmul(
            ops_[h][:, 0:512], ones_row_f[:],
            srow[0:1, 1024 + h * 512:1024 + (h + 1) * 512],
            start=True, stop=True), tS)
    W("vector", tS, tdone)
    BOREP_V = 0
    for h in range(2):
        BOREP_V = E("vector", lambda e, h=h: e.tensor_copy(
            borep[:, h * 512:(h + 1) * 512], ops_[h][:, 0:512]), vS)
    W("sync", tS, tdone)
    ldb2 = E("sync", lambda e: e.dma_start(out=srow[0:1, 1024:2048], in_=bias["bq"]), ld_c, 16)
    W("tensor", ld_c, ldb2)
    W("tensor", vS, max(BOREP_V, rcp_free[0]) if False else BOREP_V)
    tdone = None
    for c in range(NCH):
        tdone = E("tensor", lambda e, c=c: e.matmul(
            rdp[0:P, c:c + 1].bitcast(F32) if False else prp[:, c:c + 1],
            srow[0:1, 1024 + c * P:1024 + (c + 1) * P], one_f[:],
            start=True, stop=True), tS)
    W("vector", tS, tdone)
    BQCOL_V = E("vector", lambda e: e.tensor_copy(bqcol[:], prp[:, 0:NCH]), vS)

    E("sync", lambda e: e.dma_start(out=srow[0:1, 0:D], in_=xN[0:1, :]), ld_xt, 16)
    HALO_LD = E("sync", lambda e: e.dma_start(out=srow[0:1, D:2 * D], in_=xN[TE - 1:TE, :]), ld_xt, 16)
    W("vector", ld_xt, HALO_LD)
    for jh in range(2):
        E("vector", lambda e, jh=jh: e.tensor_reduce(
            srow[0:1, 2048 + jh:2049 + jh], srow[0:1, jh * D:(jh + 1) * D],
            AX.X, ALU.add), vS)
        E("vector", lambda e, jh=jh: e.tensor_mul(
            srow[0:1, jh * D:(jh + 1) * D], srow[0:1, jh * D:(jh + 1) * D],
            srow[0:1, jh * D:(jh + 1) * D]), vS)
        E("vector", lambda e, jh=jh: e.tensor_reduce(
            srow[0:1, 2050 + jh:2051 + jh], srow[0:1, jh * D:(jh + 1) * D],
            AX.X, ALU.add), vS)
    SPC4()
    E("vector", lambda e: e.tensor_scalar(
        srow[0:1, 2048:2052], srow[0:1, 2048:2052], 1.0 / D, None, ALU.mult), vS)
    SPC4()
    E("vector", lambda e: e.tensor_mul(
        srow[0:1, 2052:2054], srow[0:1, 2048:2050], srow[0:1, 2048:2050]), vS)
    SPC4()
    v_hv = E("vector", lambda e: e.scalar_tensor_tensor(
        srow[0:1, 2050:2052], srow[0:1, 2050:2052], EPS_LN,
        srow[0:1, 2052:2054], ALU.add, ALU.subtract), vS)
    W("scalar", vS, v_hv)
    s_hs = E("scalar", lambda e: e.activation(
        out=srow[0:1, 2052:2054], in_=srow[0:1, 2050:2052], func=AF.Sqrt), sS)
    W("vector", sS, s_hs)
    E("vector", lambda e: e.reciprocal(srow[0:1, 2054:2056], srow[0:1, 2052:2054]), vS)
    SPC4()
    E("vector", lambda e: e.tensor_copy(r_row[0:1, 0:1], srow[0:1, 2054:2055]), vS)
    E("vector", lambda e: e.tensor_copy(r_row[0:1, TE - 1:TE], srow[0:1, 2055:2056]), vS)
    E("vector", lambda e: e.tensor_scalar(
        nm_row[0:1, 0:1], srow[0:1, 2048:2049], -1.0, None, ALU.mult), vS)
    HALO_ST = E("vector", lambda e: e.tensor_scalar(
        nm_row[0:1, TE - 1:TE], srow[0:1, 2049:2050], -1.0, None, ALU.mult), vS)

    xb_free = [0, 0]
    xt_free = [0, 0]
    q_evac = [0, 0]
    q_stt_prev = [0]
    k1i_free = [0]
    oscl_free = [0]
    rbc_read = [0]
    attn_read = [0]
    rdp_free = [0]
    o_evac = [0]
    x1s_free = [0]
    qt2_free = [0, 0]
    x1t_spilled = [0]

    for it in range(9):
        par = it % 2
        if it < 8:
            t0, tw = it * TILE, TILE
        else:
            t0, tw = 4096, 2
        nblk = (tw + P - 1) // P

        if it >= 2:
            W("gpsimd", tS, xb_free[par])
        ldbx = E("gpsimd", lambda e, t0=t0, tw=tw, par=par: e.dma_start(
            out=xb1[:, par, :, 0:tw],
            in_=xT[:, t0:t0 + tw].rearrange("(c p) n -> p c n", p=P)), ld_xb, 16)
        if it >= 2:
            W("sync", vS, xt_free[par])
        if tw >= P:
            ldt = E("sync", lambda e, t0=t0, tw=tw, par=par: e.dma_start(
                out=xtm[:, par],
                in_=xN[t0:t0 + tw, :].rearrange("(b p) f -> p b f", p=P)), ld_xt, 16)
        else:
            ldt = E("sync", lambda e, t0=t0, tw=tw, par=par: e.dma_start(
                out=xtm[0:tw, par, 0, :], in_=xN[t0:t0 + tw, :]), ld_xt, 16)

        if it == 0:
            W("tensor", vS, max(HALO_ST, KVZ_RDY, WQ_SUM, BQCOL_V))
            W("tensor", ld_w, LDW_Q)
        W("tensor", sS, rbc_read[0])
        t_rbc = E("tensor", lambda e, t0=t0, tw=tw: e.matmul(
            rbc[:, 0:tw], ones_row_f[:], r_row[0:1, t0:t0 + tw],
            start=True, stop=True), tS)
        W("scalar", tS, t_rbc)
        s_rbc = E("scalar", lambda e, tw=tw: e.copy(rbsb[:, 0:tw], rbc[:, 0:tw]), sS)
        rbc_read[0] = s_rbc

        W("tensor", ld_xb, ldbx)
        v_q2_last = 0
        for fo in range(NCH):
            qpar = fo % 2
            W("tensor", vS, q_evac[qpar])
            for c in range(NCH):
                E("tensor", lambda e, fo=fo, c=c, par=par, tw=tw, qpar=qpar: e.matmul(
                    qps[qpar][:, 0:tw], wq_b[:, c, fo * P:(fo + 1) * P],
                    xb1[:, par, c, 0:tw], start=(c == 0), stop=False))
            t_q = E("tensor", lambda e, fo=fo, t0=t0, tw=tw, qpar=qpar: e.matmul(
                qps[qpar][:, 0:tw], wsrow_b[0:1, 2 * D + fo * P:2 * D + (fo + 1) * P],
                nm_row[0:1, t0:t0 + tw], start=False, stop=True), tS)
            W("vector", tS, t_q)
            W("vector", sS, max(qt2_free[qpar], s_rbc))
            v_q2 = E("vector", lambda e, qpar=qpar, tw=tw: e.tensor_mul(
                qt2[:, qpar, 0:tw], qps[qpar][:, 0:tw], rbsb[:, 0:tw]), vS)
            q_evac[qpar] = v_q2
            v_q2_last = v_q2
            W("scalar", vS, max(v_q2, q_stt_prev[0]))
            s_qe = E("scalar", lambda e, qpar=qpar, tw=tw: e.activation(
                out=qe[:, 0:tw], in_=qt2[:, qpar, 0:tw], func=AF.Exp,
                bias=bqcol[:, fo:fo + 1], scale=1.0), sS)
            qt2_free[qpar] = s_qe
            s_qu = E("scalar", lambda e, fo=fo, qpar=qpar, tw=tw: e.activation(
                out=qu[:, 0:tw], in_=qt2[:, qpar, 0:tw], func=AF.Relu,
                bias=bqcol[:, fo:fo + 1], scale=1.0), sS)
            qt2_free[qpar] = s_qu
            W("vector", sS, s_qu)
            q_stt_prev[0] = E("vector", lambda e, fo=fo, tw=tw: e.scalar_tensor_tensor(
                q_bf[:, fo, 0:tw], qe[:, 0:tw], 1.0, qu[:, 0:tw],
                ALU.min, ALU.add), vS)
        v_qall = vS.n
        xb_free[par] = tS.n

        W("tensor", vS, v_qall)
        for p_ in range(8):
            W("tensor", vS, attn_read[0])
            for hh in range(2):
                E("tensor", lambda e, p_=p_, hh=hh, tw=tw: e.matmul(
                    prp[hh * 64:(hh + 1) * 64, 0:tw],
                    kvp_b[:, (2 * p_ + hh) * HD:(2 * p_ + hh + 1) * HD],
                    q_bf[:, p_, 0:tw], start=True, stop=True))
            t_pd = E("tensor", lambda e, p_=p_, tw=tw: e.matmul(
                dnp[:, 0:tw], zp_b[:, 2 * p_:2 * p_ + 2],
                q_bf[:, p_, 0:tw], start=True, stop=True), tS)
            W("vector", tS, t_pd)
            E("vector", lambda e, tw=tw: e.tensor_scalar(
                rd_f[:, 0:tw], dnp[:, 0:tw], EPS_AT, None, ALU.add), vS)
            if tw < P:
                SPC4()
            E("vector", lambda e, tw=tw: e.reciprocal(rd_f[:, 0:tw], rd_f[:, 0:tw]), vS)
            if tw < P:
                SPC4()
            v_rd = E("vector", lambda e, tw=tw: e.tensor_copy(rd_b[:, 0:tw], rd_f[:, 0:tw]), vS)
            W("tensor", vS, v_rd)
            W("tensor", sS, rdp_free[0])
            t_rd = E("tensor", lambda e, tw=tw: e.matmul(
                rdp[:, 0:tw], exp2_b[:], rd_b[:, 0:tw], start=True, stop=True), tS)
            W("scalar", tS, t_rd)
            s_rd = E("scalar", lambda e, tw=tw: e.copy(rdsb[:, 0:tw], rdp[:, 0:tw]), sS)
            rdp_free[0] = s_rd
            W("vector", sS, s_rd)
            v_at = E("vector", lambda e, p_=p_, tw=tw: e.tensor_mul(
                attn_b[:, p_, 0:tw], prp[:, 0:tw], rdsb[:, 0:tw]), vS)
            attn_read[0] = v_at

        if DBG[0] and it == 7:
            W("sync", vS, vS.n)
            W("gpsimd", vS, vS.n)
            E("gpsimd", lambda e: e.dma_start(out=dbg_d[:, 5000:6024], in_=kvp_b[:]), dbg_s, 16)
            E("gpsimd", lambda e: e.dma_start(out=dbg_d[:, 6100:6116], in_=zp_b[:]), dbg_s, 16)

            E("sync", lambda e: e.dma_start(out=dbg_d[:, 14500:15012], in_=rdsb[:]), st_o, 16)
        W("tensor", ld_xt, ldt)
        if it == 0:
            W("tensor", ld_w, LDW_O)
        W("tensor", sp_x1, x1t_spilled[0])
        W("tensor", sp_k1, 0)
        for blk in range(nblk):
            bw = min(P, tw - blk * P)
            W("tensor", vS, o_evac[0])
            t_o = None
            for h in range(2):
                for c in range(NCH):
                    t_o = E("tensor", lambda e, c=c, h=h, blk=blk, bw=bw: e.matmul(
                        ops_[h][0:bw, 0:512],
                        attn_b[:, c, blk * P:blk * P + bw],
                        wo_b[:, c, h * 512:(h + 1) * 512],
                        start=(c == 0), stop=(c == NCH - 1)),
                        tS if c == NCH - 1 else None)
            W("vector", tS, t_o)
            W("vector", gS, x1s_free[0])
            v_x1a = None
            for h in range(2):
                v_x1a = E("vector", lambda e, h=h, blk=blk, par=par, bw=bw: e.tensor_add(
                    x1s[0:bw, h * 512:(h + 1) * 512], ops_[h][0:bw, 0:512],
                    xtm[0:bw, par, blk, h * 512:(h + 1) * 512]), vS)
            o_evac[0] = v_x1a
            W("gpsimd", vS, v_x1a)
            g_x1 = E("gpsimd", lambda e, blk=blk, bw=bw: e.tensor_add(
                x1t[0:bw, blk, :], x1s[0:bw, :], borep[0:bw, :]), gS)
            x1s_free[0] = g_x1
            if blk == nblk - 1:
                xt_free[par] = vS.n
            W("vector", gS, g_x1)
            E("vector", lambda e, blk=blk, bw=bw: e.tensor_reduce(
                scc[0:bw, 0:1], x1t[0:bw, blk, :], AX.X, ALU.add), vS)
            W("scalar", gS, g_x1)
            s_x2 = E("scalar", lambda e, blk=blk, bw=bw: e.activation(
                out=x1s[0:bw, :], in_=x1t[0:bw, blk, :], func=AF.Square), sS)
            W("vector", sS, s_x2)
            v_x2r = E("vector", lambda e, bw=bw: e.tensor_reduce(
                scc[0:bw, 1:2], x1s[0:bw, :], AX.X, ALU.add), vS)
            E("vector", lambda e, bw=bw: e.scalar_tensor_tensor(
                scc[0:bw, 2:3], scc[0:bw, 0:1], 1.0 / (float(D) * D),
                scc[0:bw, 0:1], ALU.mult, ALU.mult), vS)
            SPC4()
            v_var2 = E("vector", lambda e, bw=bw: e.scalar_tensor_tensor(
                scc[0:bw, 1:2], scc[0:bw, 1:2], 1.0 / D, scc[0:bw, 2:3],
                ALU.mult, ALU.subtract), vS)
            W("scalar", vS, v_var2)
            s_sq2 = E("scalar", lambda e, bw=bw: e.activation(
                out=scc[0:bw, 2:3], in_=scc[0:bw, 1:2], func=AF.Sqrt,
                bias=eps15_col[0:bw], scale=1.0), sS)
            W("vector", sS, s_sq2)
            E("vector", lambda e, bw=bw: e.reciprocal(scc[0:bw, 3:4], scc[0:bw, 2:3]), vS)
            SPC4()
            E("vector", lambda e, bw=bw: e.scalar_tensor_tensor(
                scc[0:bw, 4:5], scc[0:bw, 0:1], -1.0 / D, scc[0:bw, 3:4],
                ALU.mult, ALU.mult), vS)
            SPC4()
            E("vector", lambda e, blk=blk, bw=bw: e.tensor_scalar(
                xn2[0:bw, :], x1t[0:bw, blk, :], scc[0:bw, 3:4], scc[0:bw, 4:5],
                ALU.mult, ALU.add), vS)
            E("vector", lambda e, bw=bw: e.tensor_reduce(
                scc[0:bw, 5:6], xn2[0:bw, :], AX.X, ALU.max,
                apply_absolute_value=True), vS)
            SPC4()
            E("vector", lambda e, bw=bw: e.tensor_scalar(
                scc[0:bw, 6:7], scc[0:bw, 5:6], eps15_col[0:bw], c127_col[0:bw],
                ALU.max, ALU.mult), vS)
            gb2 = t0 // P + blk
            E("vector", lambda e, bw=bw, gb2=gb2: e.tensor_scalar(
                s1i_col[0:bw, gb2:gb2 + 1], scc[0:bw, 5:6], eps15_col[0:bw], c127_col[0:bw],
                ALU.max, ALU.mult), vS)
            SPC4()
            E("vector", lambda e, bw=bw: e.reciprocal(scc[0:bw, 6:7], scc[0:bw, 6:7]), vS)
            SPC4()
            W("vector", sS, k1i_free[0])
            v_k1i = E("vector", lambda e, bw=bw: e.tensor_scalar(
                k1i[0:bw, :], xn2[0:bw, :], scc[0:bw, 6:7], None, ALU.mult), vS)
            W("scalar", vS, v_k1i)
            s_k1b = E("scalar", lambda e, blk=blk, bw=bw: e.copy(
                k1b[0:bw, blk, :], k1i[0:bw, :]), sS)
            k1i_free[0] = s_k1b

        W("sync", sS, sS.n)
        W("sync", gS, gS.n)
        W("sync", vS, vS.n)
        if DBG[0] and it == 8:
            E("sync", lambda e: e.dma_start(out=dbg_d[5:7, 0:1024], in_=x1t[0:2, 0, :]), st_o, 16)
            E("sync", lambda e: e.dma_start(out=dbg_d[7:9, 0:1024], in_=xtm[0:2, 0, 0, :]), st_o, 16)
            E("sync", lambda e: e.dma_start(out=dbg_d[:, 25860:25862], in_=rdsb[:, 0:2]), st_o, 16)
            E("sync", lambda e: e.dma_start(
                out=dbg_d[:, 25870:25874].rearrange("p (a b) -> p a b", b=2),
                in_=qt2[:, :, 0:2]), st_o, 16)
            E("sync", lambda e: e.dma_start(out=dbg_d[9:11, 0:7], in_=scc[0:2, 0:7]), st_o, 16)
            W("gpsimd", vS, vS.n)
            W("gpsimd", sS, sS.n)
            E("gpsimd", lambda e: e.dma_start(
                out=dbg_d[:, 25800:25816].rearrange("p (a b) -> p a b", b=2),
                in_=xb1[:, 0, :, 0:2]), dbg_s, 16)
            E("gpsimd", lambda e: e.dma_start(
                out=dbg_d[:, 25820:25836].rearrange("p (a b) -> p a b", b=2),
                in_=q_bf[:, :, 0:2]), dbg_s, 16)
            E("gpsimd", lambda e: e.dma_start(
                out=dbg_d[:, 25840:25856].rearrange("p (a b) -> p a b", b=2),
                in_=attn_b[:, :, 0:2]), dbg_s, 16)
        if tw >= P:
            E("sync", lambda e, t0=t0, tw=tw: e.dma_start(
                out=x1_dram[t0:t0 + tw, :].rearrange("(b p) f -> p b f", p=P),
                in_=x1t[:]), sp_x1, 16)
            E("sync", lambda e, t0=t0, tw=tw: e.dma_start(
                out=k1_dram[t0:t0 + tw, :].rearrange("(b p) f -> p b f", p=P),
                in_=k1b[:]), sp_k1, 16)
        else:
            E("sync", lambda e, t0=t0, tw=tw: e.dma_start(
                out=x1_dram[t0:t0 + tw, :], in_=x1t[0:tw, 0, :]), sp_x1, 16)
            E("sync", lambda e, t0=t0, tw=tw: e.dma_start(
                out=k1_dram[t0:t0 + tw, :], in_=k1b[0:tw, 0, :]), sp_k1, 16)
        x1t_spilled[0] = sp_x1.n
        if it < 8:
            pass
    SP_X1 = sp_x1.n

    W("vector", sp_k1, sp_k1.n)
    v_zk = E("vector", lambda e: e.memset(k1b[:], 0.0), vS)
    W("sync", vS, v_zk)
    E("sync", lambda e: e.dma_start(
        out=k1_dram[4098:4226, :].rearrange("(b p) f -> p b f", p=P),
        in_=k1b[:, 0:1, :]), sp_k1, 16)
    E("sync", lambda e: e.dma_start(
        out=k1_dram[4226:4352, :], in_=k1b[0:126, 1, :]), sp_k1, 16)
    SP_K1 = sp_k1.n
    MASKED = vS.n

    if DBG[0]:
        W("sync", vS, vS.n)
        E("sync", lambda e: e.dma_start(out=dbg_d[0:1, 0:4098], in_=r_row[0:1, 0:4098]), st_o, 16)
        E("gpsimd", lambda e: e.dma_start(out=dbg_d[1:2, 0:4098], in_=nm_row[0:1, 0:4098]), dbg_s, 16)
        E("sync", lambda e: e.dma_start(out=dbg_d[0:P, 4104:4138], in_=s1i_col[:, :]), st_o, 16)
        E("gpsimd", lambda e: e.dma_start(out=dbg_d[2:3, 0:3072], in_=wsrow_b[0:1, :]), dbg_s, 16)
        E("sync", lambda e: e.dma_start(out=dbg_d[3:4, 0:32], in_=crow[0:1, :]), st_o, 16)
        E("sync", lambda e: e.dma_start(out=dbg_d[:, 4160:4680], in_=kvz_sb[:]), st_o, 16)
        E("sync", lambda e: e.dma_start(out=dbg_d[:, 4680:4936], in_=absb[:]), st_o, 16)
        E("sync", lambda e: e.dma_start(out=dbg_d[:, 4712:4714], in_=swc[:]), st_o, 16)
    psB.close()
    B1_T = tS.n
    B1_V = vS.n
    B1_S = sS.n
    B1_G = gS.n
    b1.close()
    ab1.close()

    b2 = ExitStack()
    w1w = sball(b2, "w1w", [P, NCH, DH], BF16)
    w2w = sball(b2, "w2w", [P, HCH, D], BF16)
    s2p_sb = sball(b2, "s2p_sb", [P, 256], F32)
    k1f = sball(b2, "k1f", [P, 2, NCH, 256], BF16)
    hd_s = sball(b2, "hd_s", [P, 3, 256], F32)
    cu_s = sball(b2, "cu_s", [P, 2, 256], F32)
    cv_s = sball(b2, "cv_s", [P, 2, 256], F32)
    h2 = sball(b2, "h2", [P, HCH, 256], F32)
    habs = sball(b2, "habs", [P, 256], F32)
    hmin = sball(b2, "hmin", [P, 256], F32)
    k2i = sball(b2, "k2i", [P, 256], I32)
    k2b = sball(b2, "k2b", [P, 2, 256], BF16)
    x1l = sball(b2, "x1l", [P, 2, 2, D], F32)
    oscl = sball(b2, "oscl", [P, D], F32)
    s1sb = sball(b2, "s1sb", [P, 256], F32)

    psC = ExitStack()
    bankA = psC.enter_context(nc.psum_tensor("bankA", [P, 512], F32))
    bankB = psC.enter_context(nc.psum_tensor("bankB", [P, 512], F32))
    bankC = psC.enter_context(nc.psum_tensor("bankC", [P, 512], F32))
    bankD = psC.enter_context(nc.psum_tensor("bankD", [P, 512], F32))
    o2p = [[psC.enter_context(nc.psum_tensor("o2p%d%d" % (tb, h), [P, 512], F32))
            for h in range(2)] for tb in range(2)]
    hps = [bankA[:, 0:256], bankB[:, 0:256]]
    mtp = bankA[:, 256:512]
    s2p = bankB[:, 256:512]
    s1ip = bankC[:, 0:256]
    s2rw = bankC[0:1, 256:512]
    s1rw = bankD[0:1, 0:384]

    def fence2(eng):
        W(eng, tS, B1_T)
        W(eng, vS, B1_V)
        W(eng, sS, B1_S)
        W(eng, gS, B1_G)

    fence2("gpsimd")
    W("gpsimd", sp_wq, SP_WQ_ALL)
    LDW_1 = E("gpsimd", lambda e: e.dma_start(
        out=w1w[:], in_=w1q_dram.rearrange("(c p) n -> p c n", p=P)), ld_w, 16)
    LDW_2 = E("gpsimd", lambda e: e.dma_start(
        out=w2w[:], in_=w2q_dram.rearrange("(c p) n -> p c n", p=P)), ld_w, 16)

    k1f_free = [0, 0]
    x1_free = [0, 0]
    oscl_free = [0]
    hd_free = [0, 0]
    cu_free = [0]
    cv_free = [0, 0]
    mt_free = [0]
    mt_t_prev = [0]
    s2p_free = [0]
    s1ip_free = [0]
    s2rw_free = [0]
    s1rw_free = [0]
    k2b_free = [0, 0]
    k2i_free = [0]
    o2_free = [[0, 0], [0, 0]]
    st_done = [0, 0]

    fence2("sync")
    fence2("scalar")

    E("vector", lambda e: e.memset(habs[:], 0.0), vS)
    E("vector", lambda e: e.memset(hmin[:], 0.0), vS)

    for j in range(NWIN):
        par = j % 2
        r0 = j * WSTRIDE
        wout = min(WSTRIDE, T - r0)
        wH = wout + 2
        nob = (wout + P - 1) // P

        if j >= 2:
            W("sync", tS, k1f_free[par])
            W("scalar", tS, k1f_free[par])
        if j == 0:
            W("sync", sp_k1, SP_K1)
            W("scalar", sp_k1, SP_K1)
        for c in range(NCH):
            for tb in range(2):
                E("sync", lambda e, c=c, tb=tb, r0=r0, par=par: e.dma_start_transpose(
                    k1f[:, par, c, tb * P:(tb + 1) * P],
                    k1_dram[r0 + tb * P: r0 + (tb + 1) * P, c * P:(c + 1) * P]),
                    ld_k1, 16)
        LD_K = ld_k1.n

        if j >= 2:
            W("sync", gS, x1_free[par])
            W("sync", st_o, st_done[par])
        if j == 0:
            W("sync", sp_x1, SP_X1)
        for tb in range(nob):
            bw = min(P, wout - tb * P)
            E("sync", lambda e, tb=tb, bw=bw, r0=r0, par=par: e.dma_start(
                out=x1l[0:bw, par, tb, :],
                in_=x1_dram[r0 + 1 + tb * P: r0 + 1 + tb * P + bw, :]), ld_x1, 16)
        LD_X = ld_x1.n

        m0 = r0 // P
        off = r0 - m0 * P
        if j == 0:
            W("tensor", vS, MASKED)
            W("tensor", ld_w, LDW_1)
        W("tensor", vS, s1rw_free[0])
        t_tr = None
        for mi in range(3):
            m = min(m0 + mi, 33)
            t_tr = E("tensor", lambda e, mi=mi, m=m: e.transpose(
                s1rw[0:1, mi * P:(mi + 1) * P], s1i_col[:, m:m + 1], ident_f[:]), tS)
        W("vector", tS, t_tr)
        v_s1row = E("vector", lambda e, off=off, wH=wH: e.tensor_scalar(
            s1r[0:1, 3 * P:3 * P + wH], s1rw[0:1, off:off + wH],
            crow[0:1, 0:1], None, ALU.mult), vS)
        s1rw_free[0] = v_s1row
        if j == 0:
            SPC4()
            v_s1row = E("vector", lambda e: e.tensor_mul(
                s1r[0:1, 3 * P:3 * P + 1], s1r[0:1, 3 * P:3 * P + 1],
                msk_t[0:1, 0:1]), vS)
        if j == NWIN - 1:
            SPC4()
            pos = 3 * P + wH - 1
            v_s1row = E("vector", lambda e, pos=pos: e.tensor_mul(
                s1r[0:1, pos:pos + 1], s1r[0:1, pos:pos + 1],
                msk_t[0:1, 1:2]), vS)
        W("tensor", vS, v_s1row)
        W("tensor", sS, s1ip_free[0])
        t_s1i = E("tensor", lambda e, wH=wH: e.matmul(
            s1ip[:, 0:wH], ones_row_f[:], s1r[0:1, 3 * P:3 * P + wH],
            start=True, stop=True), tS)
        W("scalar", tS, t_s1i)
        s_s1sb = E("scalar", lambda e, wH=wH: e.copy(s1sb[:, 0:wH], s1ip[:, 0:wH]), sS)
        s1ip_free[0] = s_s1sb
        if DBG[0] and j == 9:
            W("sync", sS, s_s1sb)
            E("sync", lambda e: e.dma_start(out=dbg_d[:, 22400:22656], in_=s1sb[:, 0:256]), st_o, 16)
            W("gpsimd", ld_k1, LD_K)
            E("gpsimd", lambda e, par=par: e.dma_start(
                out=dbg_d[:, 10400:12448].rearrange("p (a b) -> p a b", b=256),
                in_=k1f[:, par, :, :]), dbg_s, 16)

        W("tensor", ld_k1, LD_K)
        W("tensor", vS, max(mt_free[0], s2p_free[0]))
        g_habs = 0
        si_v = {}
        for k in range(HCH + 3):
            if k < HCH:
                hpar = k % 2
                W("tensor", vS, hd_free[hpar])
                t_h = None
                for c in range(NCH):
                    t_h = E("tensor", lambda e, k=k, c=c, par=par, hpar=hpar, wH=wH: e.matmul(
                        hps[hpar][:, 0:wH], w1w[:, c, k * P:(k + 1) * P],
                        k1f[:, par, c, 0:wH], start=(c == 0), stop=(c == NCH - 1)),
                        tS if c == NCH - 1 else None)
                if k == HCH - 1:
                    k1f_free[par] = tS.n
                W("vector", tS, t_h)
                if k == 0:
                    W("vector", sS, s_s1sb)
                v_dq = E("vector", lambda e, k=k, hpar=hpar, wH=wH: e.tensor_mul(
                    hd_s[:, k % 3, 0:wH], hps[hpar][:, 0:wH], s1sb[:, 0:wH]), vS)
                hd_free[hpar] = v_dq
            kb = k - 1
            if 0 <= kb < HCH:
                E("vector", lambda e, kb=kb, wout=wout: e.scalar_tensor_tensor(
                    cu_s[:, kb % 2, 0:wout], hd_s[:, kb % 3, 0:wout], cwa[:, kb:kb + 1],
                    hd_s[:, kb % 3, 1:1 + wout], ALU.mult, ALU.add), vS)
            kc = k - 2
            if 0 <= kc < HCH:
                W("vector", sS, cv_free[kc % 2])
                g_cv = E("vector", lambda e, kc=kc, wout=wout: e.scalar_tensor_tensor(
                    cv_s[:, kc % 2, 0:wout], hd_s[:, kc % 3, 2:2 + wout], cwb[:, kc:kc + 1],
                    cu_s[:, kc % 2, 0:wout], ALU.mult, ALU.add), vS)
                W("scalar", vS, g_cv)
                si_v[kc] = E("scalar", lambda e, kc=kc, wout=wout: e.activation(
                    out=h2[:, kc, 0:wout], in_=cv_s[:, kc % 2, 0:wout], func=AF.Silu,
                    bias=cbt[:, kc:kc + 1], scale=cwt[:, kc, 1:2]), sS)
                cv_free[kc % 2] = si_v[kc]
                if DBG[0] and j in (0, 9) and kc == 7:
                    W("sync", sS, si_v[7])
                    base = 6200 if j == 9 else 8300
                    E("sync", lambda e, base=base: e.dma_start(
                        out=dbg_d[:, base:base + 2032].rearrange("p (a b) -> p a b", b=254),
                        in_=h2[:, 0:8, 0:254]), st_o, 16)
            kd = k - 3
            if 0 <= kd < HCH:
                W("vector", sS, si_v[kd])
                if kd == 0:
                    W("vector", tS, mt_t_prev[0])
                    E("vector", lambda e, wout=wout: e.tensor_copy(
                        habs[:, 0:wout], h2[:, 0, 0:wout]), vS)
                    g_habs = E("vector", lambda e, wout=wout: e.tensor_copy(
                        hmin[:, 0:wout], h2[:, 0, 0:wout]), vS)
                else:
                    E("vector", lambda e, kd=kd, wout=wout: e.tensor_tensor(
                        habs[:, 0:wout], habs[:, 0:wout], h2[:, kd, 0:wout], ALU.max), vS)
                    g_habs = E("vector", lambda e, kd=kd, wout=wout: e.tensor_tensor(
                        hmin[:, 0:wout], hmin[:, 0:wout], h2[:, kd, 0:wout], ALU.min), vS)

        W("vector", vS, g_habs)
        E("vector", lambda e: e.tensor_scalar(hmin[:], hmin[:], -1.0, None, ALU.mult), vS)
        v_habs = E("vector", lambda e: e.tensor_tensor(habs[:], habs[:], hmin[:], ALU.max), vS)
        W("tensor", vS, v_habs)
        t_mt = None
        for tb in range(2):
            t_mt = E("tensor", lambda e, tb=tb: e.transpose(
                mtp[:, tb * P:(tb + 1) * P], habs[:, tb * P:(tb + 1) * P], ident_f[:]), tS)
        mt_t_prev[0] = t_mt
        W("vector", tS, t_mt)
        for tb in range(2):
            E("vector", lambda e, tb=tb: e.tensor_reduce(
                s2cc[:, 8 * tb:8 * tb + 1], mtp[:, tb * P:(tb + 1) * P], AX.X, ALU.max), vS)
        mt_free[0] = vS.n
        if j == 0:
            W("vector", ld_w, LDW_2)
        s2v = s2cc[:].rearrange("p (a b) -> p a b", b=8)
        SPC4()
        E("vector", lambda e, s2v=s2v: e.tensor_scalar(
            s2v[:, 2:4, 0:1], s2v[:, 0:2, 0:1], eps15_col[:], mw2c127[:], ALU.max, ALU.mult), vS)
        v_s2c = E("vector", lambda e, s2v=s2v: e.tensor_scalar(
            s2v[:, 4:6, 0:1], s2v[:, 0:2, 0:1], eps15_col[:], c127_col[:], ALU.max, ALU.mult), vS)
        W("tensor", vS, max(v_s2c, s2rw_free[0]))
        t_s2t = None
        for tb in range(2):
            t_s2t = E("tensor", lambda e, tb=tb: e.transpose(
                s2rw[0:1, tb * P:(tb + 1) * P], s2cc[:, 8 * (4 + tb):8 * (4 + tb) + 1], ident_f[:]), tS)
        W("vector", tS, t_s2t)
        v_s2r = E("vector", lambda e: e.tensor_copy(s2r[0:1, 0:256], s2rw[0:1, 0:256]), vS)
        s2rw_free[0] = v_s2r
        W("tensor", vS, v_s2r)
        t_s2p = E("tensor", lambda e: e.matmul(
            s2p[:, 0:256], ones_row_f[:], s2r[0:1, 0:256], start=True, stop=True), tS)
        if j == 0:
            W("tensor", ld_w, LDW_2)

        W("vector", tS, t_s2p)
        v_s2w = E("vector", lambda e: e.reciprocal(s2p_sb[:, 0:256], s2p[:, 0:256]), vS)
        s2p_free[0] = v_s2w
        v_k2_last = 0
        for oc in range(HCH):
            kpar = oc % 2
            W("vector", tS, k2b_free[kpar])
            W("vector", sS, k2i_free[0])
            v_k2i = E("vector", lambda e, oc=oc, wout=wout: e.tensor_tensor(
                k2i[:, 0:wout], h2[:, oc, 0:wout], s2p_sb[:, 0:wout], ALU.mult), vS)
            v_k2_last = v_k2i
            W("scalar", vS, v_k2i)
            s_k2b = E("scalar", lambda e, oc=oc, kpar=kpar, wout=wout: e.copy(
                k2b[:, kpar, 0:wout], k2i[:, 0:wout]), sS)
            k2i_free[0] = s_k2b
            W("tensor", sS, s_k2b)
            if oc == 0:
                for tb in range(nob):
                    for h in range(2):
                        W("tensor", sS, o2_free[tb][h])
            t_last = None
            for tb in range(nob):
                bw = min(P, wout - tb * P)
                for h in range(2):
                    t_last = E("tensor", lambda e, oc=oc, tb=tb, h=h, bw=bw, kpar=kpar: e.matmul(
                        o2p[tb][h][0:bw, 0:512],
                        k2b[:, kpar, tb * P:tb * P + bw],
                        w2w[:, oc, h * 512:(h + 1) * 512],
                        start=(oc == 0), stop=(oc == HCH - 1)),
                        tS if (tb == nob - 1 and h == 1) else None)
            k2b_free[kpar] = t_last
        for kp in range(2):
            if k2b_free[kp] is None:
                k2b_free[kp] = tS.n
        T_O2 = tS.n

        if DBG[0]:
            W("sync", vS, v_s2w)
            E("sync", lambda e, j=j: e.dma_start(
                out=dbg_d[:, 18000 + 256 * j:18256 + 256 * j], in_=habs[:]), st_o, 16)
            E("sync", lambda e, j=j: e.dma_start(
                out=dbg_d[:, 16000 + 64 * j:16048 + 64 * j], in_=s2cc[:]), st_o, 16)
        if DBG[0] and j == 2:
            E("sync", lambda e: e.dma_start(out=dbg_d[0:1, 24470:24726],
                                            in_=s1r[0:1, 3 * P:3 * P + 256]), st_o, 16)
        W("scalar", tS, T_O2)
        for tb in range(nob):
            bw = min(P, wout - tb * P)
            for h in range(2):
                W("scalar", gS, oscl_free[0])
                s_os = E("scalar", lambda e, tb=tb, h=h, bw=bw: e.activation(
                    out=oscl[0:bw, h * 512:(h + 1) * 512], in_=o2p[tb][h][0:bw, 0:512],
                    func=AF.Copy, bias=0.0, scale=s2cc[0:bw, 16 + 8 * tb:17 + 8 * tb]), sS)
                o2_free[tb][h] = s_os
                W("gpsimd", sS, s_os)
                W("gpsimd", ld_x1, LD_X)
                if j >= 2 and tb == 0 and h == 0:
                    W("gpsimd", st_o, st_done[par])
                oscl_free[0] = E("gpsimd", lambda e, tb=tb, h=h, bw=bw, par=par: e.tensor_add(
                    x1l[0:bw, par, tb, h * 512:(h + 1) * 512],
                    oscl[0:bw, h * 512:(h + 1) * 512],
                    x1l[0:bw, par, tb, h * 512:(h + 1) * 512]), gS)
            v_fin = gS.n
            W("sync", gS, v_fin)
            E("sync", lambda e, tb=tb, bw=bw, r0=r0, par=par: e.dma_start(
                out=out_d[r0 + tb * P: r0 + tb * P + bw, :],
                in_=x1l[0:bw, par, tb, :]), st_o, 16)
        x1_free[par] = gS.n
        st_done[par] = st_o.n

    W("sync", st_o, st_o.n)
    if DBG[0]:
        W("sync", dbg_s, dbg_s.n)
    W("sync", sp_x1, SP_X1)
    W("sync", sp_k1, SP_K1)
    W("sync", sp_wq, SP_WQ_ALL)

    psC.close()
    b2.close()
    semstk.close()
    es.close()

    with nc.Block() as block:
        @block.sync
        def _(e):
            for f in prog["sync"]:
                f(e)

        @block.scalar
        def _(e):
            for f in prog["scalar"]:
                f(e)

        @block.tensor
        def _(e):
            for f in prog["tensor"]:
                f(e)

        @block.vector
        def _(e):
            for f in prog["vector"]:
                f(e)

        @block.gpsimd
        def _(e):
            for f in prog["gpsimd"]:
                f(e)

    return nc


_NC_CACHE = [None]


def kernel(L0_attn_x, L0_attn_Wq, L0_attn_bq, L0_attn_Wk, L0_attn_bk,
           L0_attn_Wv, L0_attn_bv, L0_attn_Wo, L0_attn_bo,
           L0_ffn_w1, L0_ffn_conv_w, L0_ffn_conv_b, L0_ffn_w2):
    x = np.ascontiguousarray(np.asarray(L0_attn_x, dtype=np.float32))
    B, N, _ = x.shape
    assert (B, N) == (2, 16384)

    def cont(a):
        return np.ascontiguousarray(np.asarray(a, dtype=np.float32))

    wqT = np.ascontiguousarray(cont(L0_attn_Wq).T)
    wkT = np.ascontiguousarray(cont(L0_attn_Wk).T)
    wvT = np.ascontiguousarray(cont(L0_attn_Wv).T)
    woT = np.ascontiguousarray(cont(L0_attn_Wo).T)
    w1T = np.ascontiguousarray(cont(L0_ffn_w1).T)
    w2T = np.ascontiguousarray(cont(L0_ffn_w2).T)
    bq = cont(L0_attn_bq).reshape(1, D)
    bk = cont(L0_attn_bk).reshape(1, D)
    bv = cont(L0_attn_bv).reshape(1, D)
    bo = cont(L0_attn_bo).reshape(1, D)
    cw = np.ascontiguousarray(cont(L0_ffn_conv_w).reshape(HCH, P, 3).transpose(1, 0, 2))
    cb = np.ascontiguousarray(cont(L0_ffn_conv_b).reshape(HCH, P).T)

    if _NC_CACHE[0] is None:
        _NC_CACHE[0] = build_program()
    nc = _NC_CACHE[0]

    in_maps = []
    for core in range(8):
        b = core // 4
        s = (core % 4) * T
        lo = max(s - 1, 0)
        hi = min(s + T + 1, N)
        sl = x[b, lo:hi, :]
        if s - 1 < 0:
            sl = np.concatenate([sl[0:1], sl], axis=0)
        if s + T + 1 > N:
            sl = np.concatenate([sl, sl[-1:]], axis=0)
        xN_c = np.ascontiguousarray(sl)
        xT_c = np.ascontiguousarray(sl.T)
        mL = 0.0 if s == 0 else 1.0
        mR = 0.0 if s + T == N else 1.0
        msk = np.broadcast_to(np.array([mL, mR], np.float32), (P, 2)).copy()
        in_maps.append({
            "xT": xT_c, "xN": xN_c,
            "wqT": wqT, "wkT": wkT, "wvT": wvT, "woT": woT,
            "w1T": w1T, "w2T": w2T,
            "bq": bq, "bk": bk, "bv": bv, "bo": bo,
            "cw": cw, "cb": cb, "msk": msk,
        })

    res = run_bass_kernel_spmd(nc, in_maps, core_ids=list(range(8)))
    if DBG[0]:
        kernel.dbg_results = res.results
    outp = np.empty((2, 16384, D), dtype=np.float32)
    for core in range(8):
        b = core // 4
        s = (core % 4) * T
        outp[b, s:s + T, :] = res.results[core]["out"]
    return outp


# version 14
# speedup vs baseline: 1.0152x; 1.0152x over previous
from contextlib import ExitStack

import numpy as np
import ml_dtypes

import concourse.bass as bass
import concourse.mybir as mybir
from concourse.bass_utils import run_bass_kernel_spmd

F32 = mybir.dt.float32
BF16 = mybir.dt.bfloat16
I32 = mybir.dt.int32
AF = mybir.ActivationFunctionType
ALU = mybir.AluOpType
AX = mybir.AxisListType

D = 1024
DH = 4096
NH = 16
HD = 64
T = 4096
TE = T + 2
P = 128
NCH = 8
HCH = 32
TILE = 512
NT_A = 8
WSTRIDE = 254
NWIN = 17
EPS_LN = 1e-5
EPS_AT = 1e-6


class Sem:
    __slots__ = ("h", "n")

    def __init__(self, es, nc, name):
        self.h = es.enter_context(nc.semaphore(name))
        self.n = 0


DBG = [False]
NORACE = [False]


def build_program():
    nc = bass.Bass("TRN2", target_bir_lowering=False, num_devices=8,
                   dynamic_dma_scratch_size=16384,
                   detect_race_conditions=not NORACE[0])
    ikind = "ExternalOutput" if DBG[0] else "Internal"

    xT = nc.dram_tensor("xT", [D, TE], F32, kind="ExternalInput").ap()
    xN = nc.dram_tensor("xN", [TE, D], F32, kind="ExternalInput").ap()
    wts = {}
    for nm, shp in (("wqT", [D, D]), ("wkT", [D, D]), ("wvT", [D, D]),
                    ("woT", [D, D]), ("w1T", [D, DH]), ("w2T", [DH, D])):
        wts[nm] = nc.dram_tensor(nm, shp, F32, kind="ExternalInput").ap()
    bias = {}
    for nm in ("bq", "bk", "bv", "bo"):
        bias[nm] = nc.dram_tensor(nm, [1, D], F32, kind="ExternalInput").ap()
    cw_d = nc.dram_tensor("cw", [P, HCH, 3], F32, kind="ExternalInput").ap()
    cb_d = nc.dram_tensor("cb", [P, HCH], F32, kind="ExternalInput").ap()
    msk_d = nc.dram_tensor("msk", [P, 2], F32, kind="ExternalInput").ap()
    out_d = nc.dram_tensor("out", [T, D], F32, kind="ExternalOutput").ap()

    k1_dram = nc.dram_tensor("k1_dram", [4352, D], BF16, kind=ikind).ap()
    x1_dram = nc.dram_tensor("x1_dram", [TE, D], F32, kind=ikind).ap()
    w1q_dram = nc.dram_tensor("w1q_dram", [D, DH], BF16, kind=ikind).ap()
    w2q_dram = nc.dram_tensor("w2q_dram", [DH, D], BF16, kind=ikind).ap()
    cc_in = nc.dram_tensor("cc_in", [P, 520], F32, kind="Internal").ap()
    dbg_d = nc.dram_tensor("dbg", [P, 26000], F32, kind="ExternalOutput").ap() if DBG[0] else None
    cc_out = nc.dram_tensor("cc_out", [P, 520], F32, kind="Internal").ap()

    consts_np = {
        "c_ones_col_b": np.ones((P, 1), ml_dtypes.bfloat16),
        "c_ones_col_f": np.ones((P, 1), np.float32),
        "c_ones_row_f": np.ones((1, P), np.float32),
        "c_one_f": np.ones((1, 1), np.float32),
        "c_ident_f": np.eye(P, dtype=np.float32),
    }
    e2 = np.zeros((2, P), ml_dtypes.bfloat16)
    e2[0, 0:64] = 1.0
    e2[1, 64:128] = 1.0
    consts_np["c_exp2"] = e2
    consts_dram = {k: nc.inline_tensor(v, k).ap() for k, v in consts_np.items()}

    es = ExitStack()

    def sball(stk, name, shape, dt):
        return stk.enter_context(nc.sbuf_tensor(name, shape, dt))

    ones_col_b = sball(es, "ones_col_b", [P, 1], BF16)
    ones_col_f = sball(es, "ones_col_f", [P, 1], F32)
    ones_row_f = sball(es, "ones_row_f", [1, P], F32)
    one_f = sball(es, "one_f", [1, 1], F32)
    ident_f = sball(es, "ident_f", [P, P], F32)
    exp2_b = sball(es, "exp2_b", [2, P], BF16)
    crow = sball(es, "crow", [1, 32], F32)

    spacer = sball(es, "spacer", [P, 128], F32)
    s1i_col = sball(es, "s1i_col", [P, 34], F32)
    msk_t = sball(es, "msk_t", [P, 2], F32)
    cwt = sball(es, "cwt", [P, HCH, 3], F32)
    cbt = sball(es, "cbt", [P, HCH], F32)
    cwa = sball(es, "cwa", [P, HCH], F32)
    cwb = sball(es, "cwb", [P, HCH], F32)
    mw2c = sball(es, "mw2c", [P, 1], F32)
    mw2c127 = sball(es, "mw2c127", [P, 1], F32)
    eps15_col = sball(es, "eps15_col", [P, 1], F32)
    c127_col = sball(es, "c127_col", [P, 1], F32)
    swc = sball(es, "swc", [P, 2], F32)

    s1r = sball(es, "s1r", [1, 3 * P + 256 + 16], F32)
    s2r = sball(es, "s2r", [1, 256], F32)
    s2cc = sball(es, "s2cc", [P, 48], F32)
    rcol = sball(es, "rcol", [P, 8], F32)

    ab1 = ExitStack()
    r_row = sball(ab1, "r_row", [1, 4104], F32)
    absb = sball(ab1, "absb", [P, 256], F32)
    crow2 = sball(ab1, "crow2", [1, 32], F32)
    nm_row = sball(ab1, "nm_row", [1, 4104], BF16)
    wq_b = sball(ab1, "wq_b", [P, NCH, D], BF16)
    wsrow_b = sball(ab1, "wsrow_b", [1, 3 * D], BF16)
    kvz_sb = sball(ab1, "kvz_sb", [P, 520], F32)
    kvp_b = sball(ab1, "kvp_b", [P, NH * HD], BF16)
    zp_b = sball(ab1, "zp_b", [P, NH], BF16)
    srow = sball(ab1, "srow", [1, 2080], F32)

    sA_ = ExitStack()
    wk_b = sball(sA_, "wk_b", [P, NCH, D], BF16)
    wv_b = sball(sA_, "wv_b", [P, NCH, D], BF16)
    bkrep = sball(sA_, "bkrep", [P, D], F32)
    bvrep = sball(sA_, "bvrep", [P, D], F32)
    xa = sball(sA_, "xa", [P, 2, NCH, TILE], BF16)
    xsq = sball(sA_, "xsq", [P, NCH, TILE], BF16)
    k_tm = sball(sA_, "k_tm", [P, 2, D], BF16)
    v_tm = sball(sA_, "v_tm", [P, 2, NH, HD + 1], BF16)
    kvt1 = sball(sA_, "kvt1", [P, 2, D], F32)
    vvt1 = sball(sA_, "vvt1", [P, 2, D], F32)
    ke = sball(sA_, "ke", [P, 2, D], BF16)

    sW_ = ExitStack()
    DH2 = DH // 2
    wst = sball(sW_, "wst", [P, 2, DH2], F32)
    wqo = sball(sW_, "wqo", [P, DH2], BF16)
    wqi = sball(sW_, "wqi", [P, DH2], I32)

    sems = {}
    semstk = ExitStack()

    def S_(name):
        s = Sem(semstk, nc, name)
        sems[name] = s
        return s

    ld_c = S_("ld_c"); ld_w = S_("ld_w"); ld_xa = S_("ld_xa")
    ld_xb = S_("ld_xb"); ld_xt = S_("ld_xt"); ld_k1 = S_("ld_k1")
    ld_x1 = S_("ld_x1"); sp_k1 = S_("sp_k1"); sp_x1 = S_("sp_x1")
    st_o = S_("st_o"); xc_s = S_("xc_s"); cc_s = S_("cc_s")
    ld_wq = S_("ld_wq"); sp_wq = S_("sp_wq"); dbg_s = S_("dbg_s")
    tS = S_("tS"); vS = S_("vS"); sS = S_("sS"); gS = S_("gS")

    prog = {"sync": [], "scalar": [], "tensor": [], "vector": [], "gpsimd": []}

    def E(eng, f, sem=None, k=1):
        if sem is None:
            prog[eng].append(f)
            return None
        prog[eng].append(lambda e, f=f, s=sem.h, k=k: f(e).then_inc(s, k))
        sem.n += k
        return sem.n

    def SPC():
        E("vector", lambda e: e.memset(spacer[:, 0:64], 0.0), vS)

    def SPC4():
        E("vector", lambda e: e.memset(spacer[:, 0:64], 0.0), vS)
        E("vector", lambda e: e.tensor_copy(spacer[:, 64:128], spacer[:, 0:64]), vS)
        E("vector", lambda e: e.tensor_copy(spacer[:, 0:64], spacer[:, 64:128]), vS)
        E("vector", lambda e: e.tensor_copy(spacer[:, 64:128], spacer[:, 0:64]), vS)
        E("vector", lambda e: e.tensor_copy(spacer[:, 0:64], spacer[:, 64:128]), vS)

    def W(eng, sem, val):
        if val is None or val <= 0:
            return
        prog[eng].append(lambda e, s=sem.h, v=val: e.wait_ge(s, v))

    for k in consts_np:
        dst = {"c_ones_col_b": ones_col_b, "c_ones_col_f": ones_col_f,
               "c_ones_row_f": ones_row_f,
               "c_one_f": one_f, "c_ident_f": ident_f, "c_exp2": exp2_b}[k]
        E("sync", lambda e, d=dst, s=consts_dram[k]: e.dma_start(out=d[:], in_=s), ld_c, 16)
    E("sync", lambda e: e.dma_start(out=cwt[:], in_=cw_d), ld_c, 16)
    E("sync", lambda e: e.dma_start(out=cbt[:], in_=cb_d), ld_c, 16)
    LD_C = E("sync", lambda e: e.dma_start(out=msk_t[:], in_=msk_d), ld_c, 16)
    brow_q = srow

    E("gpsimd", lambda e: e.dma_start(out=wk_b[:], in_=wts["wkT"].rearrange("(c p) n -> p c n", p=P)), ld_w, 16)
    LDW_KV = E("gpsimd", lambda e: e.dma_start(out=wv_b[:], in_=wts["wvT"].rearrange("(c p) n -> p c n", p=P)), ld_w, 16)
    LDW_Q = E("gpsimd", lambda e: e.dma_start(out=wq_b[:], in_=wts["wqT"].rearrange("(c p) n -> p c n", p=P)), ld_w, 16)

    psA = ExitStack()
    kps = [psA.enter_context(nc.psum_tensor("kps0", [P, 512], F32)),
           psA.enter_context(nc.psum_tensor("kps1", [P, 512], F32))]
    vps = [psA.enter_context(nc.psum_tensor("vps0", [P, 512], F32)),
           psA.enter_context(nc.psum_tensor("vps1", [P, 512], F32))]
    kvzp = [psA.enter_context(nc.psum_tensor("kvzp0", [P, 512], F32)),
            psA.enter_context(nc.psum_tensor("kvzp1", [P, 512], F32))]
    stp = psA.enter_context(nc.psum_tensor("stp", [1, 512], F32))
    rcp = psA.enter_context(nc.psum_tensor("rcp", [P, 512], F32))

    stp_free = [0]
    rcp_free = [0]

    def wsum(w_b, slot, ldneed):
        W("tensor", ld_w, ldneed)
        vdone = None
        for h in range(2):
            W("tensor", vS, stp_free[0])
            tdone = None
            for c in range(NCH):
                tdone = E("tensor", lambda e, c=c, h=h, w=w_b: e.matmul(
                    stp[0:1, 0:512], ones_col_b[:],
                    w[:, c, h * 512:(h + 1) * 512],
                    start=(c == 0), stop=(c == NCH - 1)),
                    tS if c == NCH - 1 else None)
            W("vector", tS, tdone)
            vdone = E("vector", lambda e, s=slot, h=h: e.tensor_copy(
                wsrow_b[0:1, s * D + h * 512:s * D + (h + 1) * 512],
                stp[0:1, 0:512]), vS)
            stp_free[0] = vdone
        return vdone

    W("tensor", ld_c, LD_C)
    WK_SUM = wsum(wk_b, 0, LDW_KV)
    WV_SUM = wsum(wv_b, 1, LDW_KV)
    WQ_SUM = wsum(wq_b, 2, LDW_Q)

    REPS_V = 0
    for nm, dst in (("bk", bkrep), ("bv", bvrep)):
        ldb = E("sync", lambda e, nm=nm: e.dma_start(out=srow[0:1, 1024:2048], in_=bias[nm]), ld_c, 16)
        W("tensor", ld_c, ldb)
        tdone = None
        for h in range(2):
            tdone = E("tensor", lambda e, h=h: e.matmul(
                kps[h][:, 0:512], ones_row_f[:],
                srow[0:1, 1024 + h * 512:1024 + (h + 1) * 512],
                start=True, stop=True), tS)
        W("vector", tS, tdone)
        for h in range(2):
            REPS_V = E("vector", lambda e, d=dst, h=h: e.tensor_copy(
                d[:, h * 512:(h + 1) * 512], kps[h][:, 0:512]), vS)
        W("sync", vS, REPS_V)

    W("vector", ld_c, LD_C)
    E("vector", lambda e: e.reciprocal(cwa[:], cwt[:, :, 1]), vS)
    E("vector", lambda e: e.memset(kvz_sb[:], 0.0), vS)
    E("vector", lambda e: e.tensor_mul(cwb[:], cwa[:], cwt[:, :, 2]), vS)
    E("vector", lambda e: e.memset(absb[:], 0.0), vS)
    E("vector", lambda e: e.tensor_mul(cwa[:], cwa[:], cwt[:, :, 0]), vS)
    E("vector", lambda e: e.memset(s1i_col[:], 0.0), vS)
    E("vector", lambda e: e.memset(crow[:], 0.0), vS)
    E("vector", lambda e: e.memset(s2cc[:], 0.0), vS)
    E("vector", lambda e: e.memset(eps15_col[:], 1e-5), vS)
    E("vector", lambda e: e.memset(c127_col[:], 1.0 / 127.0), vS)
    PRESET_V = E("vector", lambda e: e.memset(v_tm[:, :, :, HD:HD + 1], 1.0), vS)

    wq_steps = []

    def emit_wq_load(step, par):
        if step < 16:
            c, hh = step // 2, step % 2
            src = wts["w1T"].rearrange("(c p) n -> p c n", p=P)[:, c, hh * DH2:(hh + 1) * DH2]
        else:
            s2 = step - 16
            src = wts["w2T"].rearrange("(c p) n -> p c n", p=P)[:, 2 * s2:2 * (s2 + 1), :]
            return E("gpsimd", lambda e, src=src, par=par: e.dma_start(
                out=wst[:, par, :].rearrange("p (c n) -> p c n", n=D), in_=src), ld_wq, 16)
        return E("gpsimd", lambda e, src=src, par=par: e.dma_start(
            out=wst[:, par, :], in_=src), ld_wq, 16)

    def emit_wq_pass1(step, par, ldval):
        W("vector", ld_wq, ldval)
        return E("vector", lambda e, step=step, par=par: e.tensor_reduce(
            absb[:, 8 * step:8 * step + 1], wst[:, par, :], AX.X, ALU.add,
            apply_absolute_value=True), vS)

    def emit_wq_scales(v_abs):
        W("tensor", vS, max(v_abs, stp_free[0]))
        t_sum = E("tensor", lambda e: e.matmul(
            stp[0:1, 0:256], ones_col_f[:], absb[:, 0:256], start=True, stop=True), tS)
        W("vector", tS, t_sum)
        crow_v = crow[:].rearrange("a (b c) -> a b c", c=8)
        E("vector", lambda e: e.tensor_reduce(crow[0:1, 0:1], stp[0:1, 0:128], AX.X, ALU.add), vS)
        sfree = E("vector", lambda e: e.tensor_reduce(crow[0:1, 8:9], stp[0:1, 128:256], AX.X, ALU.add), vS)
        stp_free[0] = sfree
        SPC4()
        E("vector", lambda e: e.tensor_scalar(crow_v[0:1, 0:2, 0:1], crow_v[0:1, 0:2, 0:1], 1.0 / (D * DH), None, ALU.mult), vS)
        SPC4()
        E("vector", lambda e: e.tensor_scalar(crow_v[0:1, 2:4, 0:1], crow_v[0:1, 0:2, 0:1], eps15_col[0:1], None, ALU.max), vS)
        SPC4()
        E("vector", lambda e: e.reciprocal(crow_v[0:1, 2:4, 0:1], crow_v[0:1, 2:4, 0:1]), vS)
        SPC4()
        v_sw = E("vector", lambda e: e.tensor_copy(crow2[0:1, :], crow[0:1, :]), vS)
        W("tensor", vS, max(v_sw, rcp_free[0]))
        t_bc = None
        for i in range(3):
            src = [crow2[0:1, 16:17], crow2[0:1, 24:25], crow2[0:1, 8:9]][i]
            dst = [rcp[:, 0:1], rcp[:, 1:2], rcp[:, 2:3]][i]
            t_bc = E("tensor", lambda e, s=src, d=dst: e.matmul(
                d, ones_row_f[:], s, start=True, stop=True), tS)
        W("vector", tS, t_bc)
        E("vector", lambda e: e.tensor_copy(swc[:, 0:2], rcp[:, 0:2]), vS)
        E("vector", lambda e: e.tensor_copy(mw2c[:], rcp[:, 2:3]), vS)
        vdone = E("vector", lambda e: e.tensor_scalar(
            mw2c127[:], rcp[:, 2:3], 1.0 / 127.0, None, ALU.mult), vS)
        rcp_free[0] = vdone
        return vdone

    def emit_wq_pass2(step, par, ldval, scales_done, prev_spill):
        W("vector", ld_wq, ldval)
        W("vector", vS, scales_done)
        sl = 0 if step < 16 else 1
        E("vector", lambda e, par=par, sl=sl: e.tensor_scalar(
            wqi[:], wst[:, par, :], swc[:, sl:sl + 1], None, ALU.mult), vS)
        W("vector", sp_wq, prev_spill)
        v_q = E("vector", lambda e: e.tensor_scalar(
            wqo[:], wqi[:], 1.0, -1.0, ALU.min, ALU.max), vS)
        W("sync", vS, v_q)
        if step < 16:
            c, hh = step // 2, step % 2
            dst = w1q_dram.rearrange("(c p) n -> p c n", p=P)[:, c, hh * DH2:(hh + 1) * DH2]
            spdone = E("sync", lambda e, dst=dst: e.dma_start(out=dst, in_=wqo[:]), sp_wq, 16)
        else:
            s2 = step - 16
            dst = w2q_dram.rearrange("(c p) n -> p c n", p=P)[:, 2 * s2:2 * (s2 + 1), :]
            spdone = E("sync", lambda e, dst=dst: e.dma_start(
                out=dst, in_=wqo[:].rearrange("p (c n) -> p c n", n=D)), sp_wq, 16)
        return spdone

    xa_free = [0, 0]
    kvz_acc_v = [0]
    prev_tstats = [0]
    kvz_t = {}
    kvt1_free = [0, 0]
    vvt1_free = [0, 0]
    rcol_free = [0, 0, 0, 0]
    pend = []
    KVZ_LAST = [0]
    kps_free = [0]
    vps_free = [0]
    v_rows_at = {}
    wq_p1 = [0, 0]
    wq_state = {"loads": [], "p1done": [], "scales": None, "p2next": 0}

    def kvz_mms(gb, par, vneed, gneed):
        W("tensor", vS, vneed)
        W("tensor", vS, kvz_acc_v[0])
        W("tensor", gS, gneed)
        r = None
        for p_ in range(8):
            bank = p_ // 4
            off = (p_ % 4) * 65
            for hh in range(2):
                h = 2 * p_ + hh
                r = E("tensor", lambda e, par=par, h=h, hh=hh, bank=bank, off=off: e.matmul(
                          kvzp[bank][hh * 64:(hh + 1) * 64, off:off + 65],
                          k_tm[:, par, h * HD:(h + 1) * HD],
                          v_tm[:, par, h, :],
                          start=True, stop=True),
                      tS if (p_ == 7 and hh == 1) else None)
        kvz_t[gb] = r
        W("vector", tS, r)
        for bank in range(2):
            kvz_acc_v[0] = E("vector", lambda e, bank=bank: e.tensor_add(
                kvz_sb[:, bank * 260:(bank + 1) * 260],
                kvz_sb[:, bank * 260:(bank + 1) * 260],
                kvzp[bank][:, 0:260]), vS)
        if gb == NT_A * 4 - 1:
            KVZ_LAST[0] = kvz_acc_v[0]

    for it in range(NT_A):
        par = it % 2
        base = 1 + it * TILE
        if it >= 2:
            W("gpsimd", tS, xa_free[par])
        lda = E("gpsimd", lambda e, it=it, par=par: e.dma_start(
            out=xa[:, par], in_=xT[:, 1 + it * TILE: 1 + (it + 1) * TILE]
            .rearrange("(c p) n -> p c n", p=P)), ld_xa, 16)

        for sub in range(4):
            step = it * 4 + sub
            wpar = step % 2
            if it < 8:
                if step < 32:
                    if step >= 2:
                        W("gpsimd", vS, wq_p1[wpar])
                    lv = emit_wq_load(step, wpar)
                    wq_state["loads"].append(lv)
                    v1 = emit_wq_pass1(step, wpar, lv)
                    wq_p1[wpar] = v1
                    wq_state["p1done"].append(v1)

        W("vector", ld_xa, lda)
        W("vector", tS, prev_tstats[0])
        v_sq = E("vector", lambda e, par=par: e.tensor_mul(xsq[:], xa[:, par], xa[:, par]), vS)

        W("tensor", ld_xa, lda)
        if it == 0:
            W("tensor", ld_w, LDW_KV)
            W("tensor", vS, max(WK_SUM, WV_SUM, REPS_V, PRESET_V))
        W("tensor", vS, max(v_sq, stp_free[0]))
        t_s1 = None
        for c in range(NCH):
            t_s1 = E("tensor", lambda e, c=c, par=par: e.matmul(
                stp[0:1, 0:512], ones_col_b[:], xa[:, par, c, :],
                start=(c == 0), stop=(c == NCH - 1)),
                tS if c == NCH - 1 else None)
        W("vector", tS, t_s1)
        v_m = E("vector", lambda e: e.tensor_scalar(srow[0:1, 0:512], stp[0:1, 0:512], 1.0 / D, None, ALU.mult), vS)
        W("tensor", vS, v_m)
        t_stats = None
        for c in range(NCH):
            t_stats = E("tensor", lambda e, c=c: e.matmul(
                stp[0:1, 0:512], ones_col_b[:], xsq[:, c, :],
                start=(c == 0), stop=(c == NCH - 1)),
                tS if c == NCH - 1 else None)
        prev_tstats[0] = t_stats
        W("vector", tS, t_stats)
        v_stfree = E("vector", lambda e: e.tensor_scalar(srow[0:1, 512:1024], stp[0:1, 0:512], 1.0 / D, None, ALU.mult), vS)
        stp_free[0] = v_stfree
        E("vector", lambda e: e.tensor_mul(srow[0:1, 1024:1536], srow[0:1, 0:512], srow[0:1, 0:512]), vS)
        v_var = E("vector", lambda e: e.scalar_tensor_tensor(
            srow[0:1, 512:1024], srow[0:1, 512:1024], EPS_LN, srow[0:1, 1024:1536],
            ALU.add, ALU.subtract), vS)
        W("scalar", vS, v_var)
        s_sq = E("scalar", lambda e: e.activation(out=srow[0:1, 1024:1536], in_=srow[0:1, 512:1024], func=AF.Sqrt), sS)
        W("vector", sS, s_sq)
        E("vector", lambda e, base=base: e.reciprocal(r_row[0:1, base:base + TILE], srow[0:1, 1024:1536]), vS)
        v_rows = E("vector", lambda e, base=base: e.tensor_scalar(
            nm_row[0:1, base:base + TILE], srow[0:1, 0:512], -1.0, None, ALU.mult), vS)

        for blk in range(4):
            gb = it * 4 + blk
            bpar = gb % 2
            if pend and pend[0][0] == gb - 2:
                pg = pend.pop(0)
                kvz_mms(pg[0], pg[1], pg[2], pg[3])
            W("tensor", sS, kps_free[0])
            for c in range(NCH):
                for h in range(2):
                    E("tensor", lambda e, c=c, h=h, par=par, blk=blk: e.matmul(
                        kps[h][:, 0:512], xa[:, par, c, blk * P:(blk + 1) * P],
                        wk_b[:, c, h * 512:(h + 1) * 512],
                        start=(c == 0), stop=False))
            if blk == 0:
                W("tensor", vS, v_rows)
            t_k = None
            for h in range(2):
                t_k = E("tensor", lambda e, h=h, base=base, blk=blk: e.matmul(
                    kps[h][:, 0:512],
                    nm_row[0:1, base + blk * P: base + (blk + 1) * P],
                    wsrow_b[0:1, h * 512:(h + 1) * 512],
                    start=False, stop=True), tS)
            W("tensor", sS, vps_free[0])
            t_v = None
            for c in range(NCH):
                for h in range(2):
                    E("tensor", lambda e, c=c, h=h, par=par, blk=blk: e.matmul(
                        vps[h][:, 0:512], xa[:, par, c, blk * P:(blk + 1) * P],
                        wv_b[:, c, h * 512:(h + 1) * 512],
                        start=(c == 0), stop=False))
            for h in range(2):
                t_v = E("tensor", lambda e, h=h, base=base, blk=blk: e.matmul(
                    vps[h][:, 0:512],
                    nm_row[0:1, base + blk * P: base + (blk + 1) * P],
                    wsrow_b[0:1, D + h * 512: D + (h + 1) * 512],
                    start=False, stop=True), tS)
            if blk == 3:
                xa_free[par] = tS.n
            W("tensor", vS, rcp_free[0])
            t_rc = E("tensor", lambda e, base=base, blk=blk: e.matmul(
                rcp[:, blk:blk + 1],
                r_row[0:1, base + blk * P: base + (blk + 1) * P], one_f[:],
                start=True, stop=True), tS)

            W("vector", tS, t_rc)
            W("vector", sS, rcol_free[blk])
            v_rc = E("vector", lambda e, blk=blk: e.tensor_copy(
                rcol[:, blk:blk + 1], rcp[:, blk:blk + 1]), vS)
            rcp_free[0] = v_rc
            W("scalar", tS, t_k)
            W("scalar", vS, max(v_rc, kvt1_free[bpar]))
            s_k1 = None
            for h in range(2):
                s_k1 = E("scalar", lambda e, h=h, bpar=bpar, blk=blk: e.activation(
                    out=kvt1[:, bpar, h * 512:(h + 1) * 512], in_=kps[h][:, 0:512],
                    func=AF.Copy, bias=0.0, scale=rcol[:, blk:blk + 1]), sS)
            kps_free[0] = s_k1
            W("scalar", tS, t_v)
            W("scalar", gS, vvt1_free[bpar])
            s_v1 = None
            for h in range(2):
                s_v1 = E("scalar", lambda e, h=h, bpar=bpar, blk=blk: e.activation(
                    out=vvt1[:, bpar, h * 512:(h + 1) * 512], in_=vps[h][:, 0:512],
                    func=AF.Copy, bias=0.0, scale=rcol[:, blk:blk + 1]), sS)
            vps_free[0] = s_v1
            W("vector", sS, s_k1)
            v_kz = E("vector", lambda e, bpar=bpar: e.tensor_add(
                kvt1[:, bpar, :], kvt1[:, bpar, :], bkrep[:]), vS)
            W("scalar", vS, v_kz)
            s_ke = E("scalar", lambda e, bpar=bpar: e.activation(
                out=ke[:, bpar, :], in_=kvt1[:, bpar, :], func=AF.Exp), sS)
            W("gpsimd", vS, v_kz)
            W("gpsimd", sS, s_ke)
            g_ku = E("gpsimd", lambda e, bpar=bpar: e.tensor_scalar(
                kvt1[:, bpar, :], kvt1[:, bpar, :], 0.0, None, ALU.max), gS)
            W("vector", sS, s_ke)
            W("vector", gS, g_ku)
            W("vector", tS, kvz_t.get(gb - 2, 0))
            v_k = E("vector", lambda e, bpar=bpar: e.scalar_tensor_tensor(
                k_tm[:, bpar, :], ke[:, bpar, :], 1.0, kvt1[:, bpar, :],
                ALU.min, ALU.add), vS)
            W("gpsimd", sS, s_v1)
            W("gpsimd", tS, kvz_t.get(gb - 2, 0))
            g_v = E("gpsimd", lambda e, bpar=bpar: e.tensor_add(
                v_tm[:, bpar, :, 0:HD],
                vvt1[:, bpar].rearrange("p (h c) -> p h c", c=HD),
                bvrep[:].rearrange("p (h c) -> p h c", c=HD)), gS)
            kvt1_free[bpar] = v_k
            vvt1_free[bpar] = g_v
            rcol_free[blk] = s_v1
            pend.append((gb, bpar, v_k, g_v))

    while pend:
        pg = pend.pop(0)
        kvz_mms(pg[0], pg[1], pg[2], pg[3])

    if DBG[0]:
        W("sync", vS, wq_state["p1done"][-1])
        E("sync", lambda e: e.dma_start(out=dbg_d[:, 25000:25256], in_=absb[:, 0:256]), st_o, 16)
    SCALES_V = emit_wq_scales(wq_state["p1done"][-1])
    if DBG[0]:
        W("sync", vS, SCALES_V)
        E("sync", lambda e: e.dma_start(out=dbg_d[4:5, 25700:25732], in_=crow[0:1, :]), st_o, 16)
        W("sync", vS, vS.n)
        E("sync", lambda e: e.dma_start(out=dbg_d[4:5, 1:4097], in_=r_row[0:1, 1:4097]), st_o, 16)
    wq_p2 = [wq_p1[0], wq_p1[1]]
    SP_WQ_ALL = 0
    for step in range(32):
        wpar = step % 2
        W("gpsimd", vS, wq_p2[wpar])
        lv = emit_wq_load(step, wpar)
        sp = emit_wq_pass2(step, wpar, lv,
                           SCALES_V if step == 0 else 0, SP_WQ_ALL)
        wq_p2[wpar] = vS.n
        SP_WQ_ALL = sp

    v_kvz = KVZ_LAST[0]
    W("sync", vS, v_kvz)
    E("sync", lambda e: e.dma_start(out=cc_in, in_=kvz_sb[:]), xc_s, 16)
    W("gpsimd", xc_s, xc_s.n)
    E("gpsimd", lambda e: e.collective_compute(
        "AllReduce", ALU.add, replica_groups=[[0, 1, 2, 3], [4, 5, 6, 7]],
        ins=[cc_in], outs=[cc_out]), cc_s, 1)
    W("sync", cc_s, cc_s.n)
    XC_D = E("sync", lambda e: e.dma_start(out=kvz_sb[:], in_=cc_out), xc_s, 16)
    if DBG[0]:
        W("sync", xc_s, XC_D)
        E("sync", lambda e: e.dma_start(out=dbg_d[:, 25100:25620], in_=kvz_sb[:]), st_o, 16)
    W("vector", xc_s, XC_D)
    E("vector", lambda e: e.memset(kvp_b[:], 0.0), vS)
    E("vector", lambda e: e.memset(zp_b[:], 0.0), vS)
    KVZ_RDY = 0
    for p_ in range(8):
        E("vector", lambda e, p_=p_: e.tensor_copy(
            kvp_b[0:64, 2 * p_ * HD:(2 * p_ + 1) * HD],
            kvz_sb[0:64, p_ * 65:p_ * 65 + 64]), vS)
        E("vector", lambda e, p_=p_: e.tensor_copy(
            kvp_b[64:128, (2 * p_ + 1) * HD:(2 * p_ + 2) * HD],
            kvz_sb[64:128, p_ * 65:p_ * 65 + 64]), vS)
        E("vector", lambda e, p_=p_: e.tensor_copy(
            zp_b[0:64, 2 * p_:2 * p_ + 1], kvz_sb[0:64, p_ * 65 + 64:p_ * 65 + 65]), vS)
        KVZ_RDY = E("vector", lambda e, p_=p_: e.tensor_copy(
            zp_b[64:128, 2 * p_ + 1:2 * p_ + 2], kvz_sb[64:128, p_ * 65 + 64:p_ * 65 + 65]), vS)

    psA.close()

    A_T = tS.n
    A_V = vS.n
    A_S = sS.n
    A_G = gS.n

    sW_.close()
    sA_.close()

    b1 = ExitStack()
    wo_b = sball(b1, "wo_b", [P, NCH, D], BF16)
    borep = sball(b1, "borep", [P, D], F32)
    bqcol = sball(b1, "bqcol", [P, NCH], F32)
    xb1 = sball(b1, "xb1", [P, 2, NCH, TILE], BF16)
    xtm = sball(b1, "xtm", [P, 2, 4, D], F32)
    q_bf = sball(b1, "q_bf", [P, NCH, TILE], BF16)
    qt2 = sball(b1, "qt2", [P, 2, TILE], F32)
    qe = sball(b1, "qe", [P, TILE], BF16)
    qu = sball(b1, "qu", [P, TILE], F32)
    rd_f = sball(b1, "rd_f", [2, TILE], F32)
    rd_b = sball(b1, "rd_b", [2, TILE], BF16)
    rdsb = sball(b1, "rdsb", [P, TILE], F32)
    rbsb = sball(b1, "rbsb", [P, TILE], F32)
    attn_b = sball(b1, "attn_b", [P, NCH, TILE], BF16)
    x1t = sball(b1, "x1t", [P, 4, D], F32)
    x1s = sball(b1, "x1s", [P, D], F32)
    xn2 = sball(b1, "xn2", [P, D], F32)
    k1i = sball(b1, "k1i", [P, D], I32)
    k1b = sball(b1, "k1b", [P, 4, D], BF16)
    scc = sball(b1, "scc", [P, 8], F32)

    psB = ExitStack()
    qps = [psB.enter_context(nc.psum_tensor("qps0", [P, 512], F32)),
           psB.enter_context(nc.psum_tensor("qps1", [P, 512], F32))]
    rbc = psB.enter_context(nc.psum_tensor("rbc", [P, 512], F32))
    prp = psB.enter_context(nc.psum_tensor("prp", [P, 512], F32))
    dnp = psB.enter_context(nc.psum_tensor("dnp", [2, 512], F32))
    rdp = psB.enter_context(nc.psum_tensor("rdp", [P, 512], F32))
    ops_ = [psB.enter_context(nc.psum_tensor("ops0", [P, 512], F32)),
            psB.enter_context(nc.psum_tensor("ops1", [P, 512], F32))]

    def fence(eng):
        W(eng, tS, A_T)
        W(eng, vS, A_V)
        W(eng, sS, A_S)
        W(eng, gS, A_G)
        W(eng, sp_wq, SP_WQ_ALL)

    fence("gpsimd")
    fence("sync")

    LDW_O = E("gpsimd", lambda e: e.dma_start(out=wo_b[:], in_=wts["woT"].rearrange("(c p) n -> p c n", p=P)), ld_w, 16)
    ldb = E("sync", lambda e: e.dma_start(out=srow[0:1, 1024:2048], in_=bias["bo"]), ld_c, 16)
    W("tensor", ld_c, ldb)
    W("tensor", vS, A_V)
    psB_first = None
    tdone = None
    for h in range(2):
        tdone = E("tensor", lambda e, h=h: e.matmul(
            ops_[h][:, 0:512], ones_row_f[:],
            srow[0:1, 1024 + h * 512:1024 + (h + 1) * 512],
            start=True, stop=True), tS)
    W("vector", tS, tdone)
    BOREP_V = 0
    for h in range(2):
        BOREP_V = E("vector", lambda e, h=h: e.tensor_copy(
            borep[:, h * 512:(h + 1) * 512], ops_[h][:, 0:512]), vS)
    W("sync", tS, tdone)
    ldb2 = E("sync", lambda e: e.dma_start(out=srow[0:1, 1024:2048], in_=bias["bq"]), ld_c, 16)
    W("tensor", ld_c, ldb2)
    W("tensor", vS, max(BOREP_V, rcp_free[0]) if False else BOREP_V)
    tdone = None
    for c in range(NCH):
        tdone = E("tensor", lambda e, c=c: e.matmul(
            rdp[0:P, c:c + 1].bitcast(F32) if False else prp[:, c:c + 1],
            srow[0:1, 1024 + c * P:1024 + (c + 1) * P], one_f[:],
            start=True, stop=True), tS)
    W("vector", tS, tdone)
    BQCOL_V = E("vector", lambda e: e.tensor_copy(bqcol[:], prp[:, 0:NCH]), vS)

    E("sync", lambda e: e.dma_start(out=srow[0:1, 0:D], in_=xN[0:1, :]), ld_xt, 16)
    HALO_LD = E("sync", lambda e: e.dma_start(out=srow[0:1, D:2 * D], in_=xN[TE - 1:TE, :]), ld_xt, 16)
    W("vector", ld_xt, HALO_LD)
    for jh in range(2):
        E("vector", lambda e, jh=jh: e.tensor_reduce(
            srow[0:1, 2048 + jh:2049 + jh], srow[0:1, jh * D:(jh + 1) * D],
            AX.X, ALU.add), vS)
        E("vector", lambda e, jh=jh: e.tensor_mul(
            srow[0:1, jh * D:(jh + 1) * D], srow[0:1, jh * D:(jh + 1) * D],
            srow[0:1, jh * D:(jh + 1) * D]), vS)
        E("vector", lambda e, jh=jh: e.tensor_reduce(
            srow[0:1, 2050 + jh:2051 + jh], srow[0:1, jh * D:(jh + 1) * D],
            AX.X, ALU.add), vS)
    SPC4()
    E("vector", lambda e: e.tensor_scalar(
        srow[0:1, 2048:2052], srow[0:1, 2048:2052], 1.0 / D, None, ALU.mult), vS)
    SPC4()
    E("vector", lambda e: e.tensor_mul(
        srow[0:1, 2052:2054], srow[0:1, 2048:2050], srow[0:1, 2048:2050]), vS)
    SPC4()
    v_hv = E("vector", lambda e: e.scalar_tensor_tensor(
        srow[0:1, 2050:2052], srow[0:1, 2050:2052], EPS_LN,
        srow[0:1, 2052:2054], ALU.add, ALU.subtract), vS)
    W("scalar", vS, v_hv)
    s_hs = E("scalar", lambda e: e.activation(
        out=srow[0:1, 2052:2054], in_=srow[0:1, 2050:2052], func=AF.Sqrt), sS)
    W("vector", sS, s_hs)
    E("vector", lambda e: e.reciprocal(srow[0:1, 2054:2056], srow[0:1, 2052:2054]), vS)
    SPC4()
    E("vector", lambda e: e.tensor_copy(r_row[0:1, 0:1], srow[0:1, 2054:2055]), vS)
    E("vector", lambda e: e.tensor_copy(r_row[0:1, TE - 1:TE], srow[0:1, 2055:2056]), vS)
    E("vector", lambda e: e.tensor_scalar(
        nm_row[0:1, 0:1], srow[0:1, 2048:2049], -1.0, None, ALU.mult), vS)
    HALO_ST = E("vector", lambda e: e.tensor_scalar(
        nm_row[0:1, TE - 1:TE], srow[0:1, 2049:2050], -1.0, None, ALU.mult), vS)

    xb_free = [0, 0]
    xt_free = [0, 0]
    q_evac = [0, 0]
    q_stt_prev = [0]
    k1i_free = [0]
    oscl_free = [0]
    rbc_read = [0]
    attn_read = [0]
    rdp_free = [0]
    o_evac = [0]
    x1s_free = [0]
    qt2_free = [0, 0]
    x1t_spilled = [0]

    ldbx_val = {}

    def emit_ldbx(itx):
        parx = itx % 2
        t0x, twx = (itx * TILE, TILE) if itx < 8 else (4096, 2)
        if itx >= 2:
            W("gpsimd", tS, xb_free[parx])
        ldbx_val[itx] = E("gpsimd", lambda e, t0=t0x, tw=twx, par=parx: e.dma_start(
            out=xb1[:, par, :, 0:tw],
            in_=xT[:, t0:t0 + tw].rearrange("(c p) n -> p c n", p=P)), ld_xb, 16)

    for it in range(9):
        par = it % 2
        if it < 8:
            t0, tw = it * TILE, TILE
        else:
            t0, tw = 4096, 2
        nblk = (tw + P - 1) // P

        if it == 0:
            emit_ldbx(0)
            emit_ldbx(1)
        elif it < 8:
            emit_ldbx(it + 1)
        ldbx = ldbx_val[it]
        if it >= 2:
            W("sync", vS, xt_free[par])
        if tw >= P:
            ldt = E("sync", lambda e, t0=t0, tw=tw, par=par: e.dma_start(
                out=xtm[:, par],
                in_=xN[t0:t0 + tw, :].rearrange("(b p) f -> p b f", p=P)), ld_xt, 16)
        else:
            ldt = E("sync", lambda e, t0=t0, tw=tw, par=par: e.dma_start(
                out=xtm[0:tw, par, 0, :], in_=xN[t0:t0 + tw, :]), ld_xt, 16)

        if it == 0:
            W("tensor", vS, max(HALO_ST, KVZ_RDY, WQ_SUM, BQCOL_V))
            W("tensor", ld_w, LDW_Q)
        W("tensor", sS, rbc_read[0])
        t_rbc = E("tensor", lambda e, t0=t0, tw=tw: e.matmul(
            rbc[:, 0:tw], ones_row_f[:], r_row[0:1, t0:t0 + tw],
            start=True, stop=True), tS)
        W("scalar", tS, t_rbc)
        s_rbc = E("scalar", lambda e, tw=tw: e.copy(rbsb[:, 0:tw], rbc[:, 0:tw]), sS)
        rbc_read[0] = s_rbc

        W("tensor", ld_xb, ldbx)
        v_q2_last = 0
        for fo in range(NCH):
            qpar = fo % 2
            W("tensor", vS, q_evac[qpar])
            for c in range(NCH):
                E("tensor", lambda e, fo=fo, c=c, par=par, tw=tw, qpar=qpar: e.matmul(
                    qps[qpar][:, 0:tw], wq_b[:, c, fo * P:(fo + 1) * P],
                    xb1[:, par, c, 0:tw], start=(c == 0), stop=False))
            t_q = E("tensor", lambda e, fo=fo, t0=t0, tw=tw, qpar=qpar: e.matmul(
                qps[qpar][:, 0:tw], wsrow_b[0:1, 2 * D + fo * P:2 * D + (fo + 1) * P],
                nm_row[0:1, t0:t0 + tw], start=False, stop=True), tS)
            W("vector", tS, t_q)
            W("vector", sS, max(qt2_free[qpar], s_rbc))
            v_q2 = E("vector", lambda e, qpar=qpar, tw=tw: e.tensor_mul(
                qt2[:, qpar, 0:tw], qps[qpar][:, 0:tw], rbsb[:, 0:tw]), vS)
            q_evac[qpar] = v_q2
            v_q2_last = v_q2
            W("scalar", vS, max(v_q2, q_stt_prev[0]))
            s_qe = E("scalar", lambda e, qpar=qpar, tw=tw: e.activation(
                out=qe[:, 0:tw], in_=qt2[:, qpar, 0:tw], func=AF.Exp,
                bias=bqcol[:, fo:fo + 1], scale=1.0), sS)
            qt2_free[qpar] = s_qe
            s_qu = E("scalar", lambda e, fo=fo, qpar=qpar, tw=tw: e.activation(
                out=qu[:, 0:tw], in_=qt2[:, qpar, 0:tw], func=AF.Relu,
                bias=bqcol[:, fo:fo + 1], scale=1.0), sS)
            qt2_free[qpar] = s_qu
            W("vector", sS, s_qu)
            q_stt_prev[0] = E("vector", lambda e, fo=fo, tw=tw: e.scalar_tensor_tensor(
                q_bf[:, fo, 0:tw], qe[:, 0:tw], 1.0, qu[:, 0:tw],
                ALU.min, ALU.add), vS)
        v_qall = vS.n
        xb_free[par] = tS.n

        W("tensor", vS, v_qall)
        for p_ in range(8):
            W("tensor", vS, attn_read[0])
            for hh in range(2):
                E("tensor", lambda e, p_=p_, hh=hh, tw=tw: e.matmul(
                    prp[hh * 64:(hh + 1) * 64, 0:tw],
                    kvp_b[:, (2 * p_ + hh) * HD:(2 * p_ + hh + 1) * HD],
                    q_bf[:, p_, 0:tw], start=True, stop=True))
            t_pd = E("tensor", lambda e, p_=p_, tw=tw: e.matmul(
                dnp[:, 0:tw], zp_b[:, 2 * p_:2 * p_ + 2],
                q_bf[:, p_, 0:tw], start=True, stop=True), tS)
            W("vector", tS, t_pd)
            E("vector", lambda e, tw=tw: e.tensor_scalar(
                rd_f[:, 0:tw], dnp[:, 0:tw], EPS_AT, None, ALU.add), vS)
            if tw < P:
                SPC4()
            E("vector", lambda e, tw=tw: e.reciprocal(rd_f[:, 0:tw], rd_f[:, 0:tw]), vS)
            if tw < P:
                SPC4()
            v_rd = E("vector", lambda e, tw=tw: e.tensor_copy(rd_b[:, 0:tw], rd_f[:, 0:tw]), vS)
            W("tensor", vS, v_rd)
            W("tensor", sS, rdp_free[0])
            t_rd = E("tensor", lambda e, tw=tw: e.matmul(
                rdp[:, 0:tw], exp2_b[:], rd_b[:, 0:tw], start=True, stop=True), tS)
            W("scalar", tS, t_rd)
            s_rd = E("scalar", lambda e, tw=tw: e.copy(rdsb[:, 0:tw], rdp[:, 0:tw]), sS)
            rdp_free[0] = s_rd
            W("vector", sS, s_rd)
            v_at = E("vector", lambda e, p_=p_, tw=tw: e.tensor_mul(
                attn_b[:, p_, 0:tw], prp[:, 0:tw], rdsb[:, 0:tw]), vS)
            attn_read[0] = v_at

        if DBG[0] and it == 7:
            W("sync", vS, vS.n)
            W("gpsimd", vS, vS.n)
            E("gpsimd", lambda e: e.dma_start(out=dbg_d[:, 5000:6024], in_=kvp_b[:]), dbg_s, 16)
            E("gpsimd", lambda e: e.dma_start(out=dbg_d[:, 6100:6116], in_=zp_b[:]), dbg_s, 16)

            E("sync", lambda e: e.dma_start(out=dbg_d[:, 14500:15012], in_=rdsb[:]), st_o, 16)
        W("tensor", ld_xt, ldt)
        if it == 0:
            W("tensor", ld_w, LDW_O)
        W("tensor", sp_x1, x1t_spilled[0])
        W("tensor", sp_k1, 0)
        for blk in range(nblk):
            bw = min(P, tw - blk * P)
            W("tensor", vS, o_evac[0])
            t_o = None
            for h in range(2):
                for c in range(NCH):
                    t_o = E("tensor", lambda e, c=c, h=h, blk=blk, bw=bw: e.matmul(
                        ops_[h][0:bw, 0:512],
                        attn_b[:, c, blk * P:blk * P + bw],
                        wo_b[:, c, h * 512:(h + 1) * 512],
                        start=(c == 0), stop=(c == NCH - 1)),
                        tS if c == NCH - 1 else None)
            W("vector", tS, t_o)
            W("vector", gS, x1s_free[0])
            v_x1a = None
            for h in range(2):
                v_x1a = E("vector", lambda e, h=h, blk=blk, par=par, bw=bw: e.tensor_add(
                    x1s[0:bw, h * 512:(h + 1) * 512], ops_[h][0:bw, 0:512],
                    xtm[0:bw, par, blk, h * 512:(h + 1) * 512]), vS)
            o_evac[0] = v_x1a
            W("gpsimd", vS, v_x1a)
            g_x1 = E("gpsimd", lambda e, blk=blk, bw=bw: e.tensor_add(
                x1t[0:bw, blk, :], x1s[0:bw, :], borep[0:bw, :]), gS)
            x1s_free[0] = g_x1
            if blk == nblk - 1:
                xt_free[par] = vS.n
            W("vector", gS, g_x1)
            E("vector", lambda e, blk=blk, bw=bw: e.tensor_reduce(
                scc[0:bw, 0:1], x1t[0:bw, blk, :], AX.X, ALU.add), vS)
            W("scalar", gS, g_x1)
            s_x2 = E("scalar", lambda e, blk=blk, bw=bw: e.activation(
                out=x1s[0:bw, :], in_=x1t[0:bw, blk, :], func=AF.Square), sS)
            W("vector", sS, s_x2)
            v_x2r = E("vector", lambda e, bw=bw: e.tensor_reduce(
                scc[0:bw, 1:2], x1s[0:bw, :], AX.X, ALU.add), vS)
            E("vector", lambda e, bw=bw: e.scalar_tensor_tensor(
                scc[0:bw, 2:3], scc[0:bw, 0:1], 1.0 / (float(D) * D),
                scc[0:bw, 0:1], ALU.mult, ALU.mult), vS)
            SPC4()
            v_var2 = E("vector", lambda e, bw=bw: e.scalar_tensor_tensor(
                scc[0:bw, 1:2], scc[0:bw, 1:2], 1.0 / D, scc[0:bw, 2:3],
                ALU.mult, ALU.subtract), vS)
            W("scalar", vS, v_var2)
            s_sq2 = E("scalar", lambda e, bw=bw: e.activation(
                out=scc[0:bw, 2:3], in_=scc[0:bw, 1:2], func=AF.Sqrt,
                bias=eps15_col[0:bw], scale=1.0), sS)
            W("vector", sS, s_sq2)
            E("vector", lambda e, bw=bw: e.reciprocal(scc[0:bw, 3:4], scc[0:bw, 2:3]), vS)
            SPC4()
            E("vector", lambda e, bw=bw: e.scalar_tensor_tensor(
                scc[0:bw, 4:5], scc[0:bw, 0:1], -1.0 / D, scc[0:bw, 3:4],
                ALU.mult, ALU.mult), vS)
            SPC4()
            E("vector", lambda e, blk=blk, bw=bw: e.tensor_scalar(
                xn2[0:bw, :], x1t[0:bw, blk, :], scc[0:bw, 3:4], scc[0:bw, 4:5],
                ALU.mult, ALU.add), vS)
            E("vector", lambda e, bw=bw: e.tensor_reduce(
                scc[0:bw, 5:6], xn2[0:bw, :], AX.X, ALU.max,
                apply_absolute_value=True), vS)
            SPC4()
            E("vector", lambda e, bw=bw: e.tensor_scalar(
                scc[0:bw, 6:7], scc[0:bw, 5:6], eps15_col[0:bw], c127_col[0:bw],
                ALU.max, ALU.mult), vS)
            gb2 = t0 // P + blk
            E("vector", lambda e, bw=bw, gb2=gb2: e.tensor_scalar(
                s1i_col[0:bw, gb2:gb2 + 1], scc[0:bw, 5:6], eps15_col[0:bw], c127_col[0:bw],
                ALU.max, ALU.mult), vS)
            SPC4()
            E("vector", lambda e, bw=bw: e.reciprocal(scc[0:bw, 6:7], scc[0:bw, 6:7]), vS)
            SPC4()
            W("vector", sS, k1i_free[0])
            v_k1i = E("vector", lambda e, bw=bw: e.tensor_scalar(
                k1i[0:bw, :], xn2[0:bw, :], scc[0:bw, 6:7], None, ALU.mult), vS)
            W("scalar", vS, v_k1i)
            s_k1b = E("scalar", lambda e, blk=blk, bw=bw: e.copy(
                k1b[0:bw, blk, :], k1i[0:bw, :]), sS)
            k1i_free[0] = s_k1b

        W("sync", sS, sS.n)
        W("sync", gS, gS.n)
        W("sync", vS, vS.n)
        if DBG[0] and it == 8:
            E("sync", lambda e: e.dma_start(out=dbg_d[5:7, 0:1024], in_=x1t[0:2, 0, :]), st_o, 16)
            E("sync", lambda e: e.dma_start(out=dbg_d[7:9, 0:1024], in_=xtm[0:2, 0, 0, :]), st_o, 16)
            E("sync", lambda e: e.dma_start(out=dbg_d[:, 25860:25862], in_=rdsb[:, 0:2]), st_o, 16)
            E("sync", lambda e: e.dma_start(
                out=dbg_d[:, 25870:25874].rearrange("p (a b) -> p a b", b=2),
                in_=qt2[:, :, 0:2]), st_o, 16)
            E("sync", lambda e: e.dma_start(out=dbg_d[9:11, 0:7], in_=scc[0:2, 0:7]), st_o, 16)
            W("gpsimd", vS, vS.n)
            W("gpsimd", sS, sS.n)
            E("gpsimd", lambda e: e.dma_start(
                out=dbg_d[:, 25800:25816].rearrange("p (a b) -> p a b", b=2),
                in_=xb1[:, 0, :, 0:2]), dbg_s, 16)
            E("gpsimd", lambda e: e.dma_start(
                out=dbg_d[:, 25820:25836].rearrange("p (a b) -> p a b", b=2),
                in_=q_bf[:, :, 0:2]), dbg_s, 16)
            E("gpsimd", lambda e: e.dma_start(
                out=dbg_d[:, 25840:25856].rearrange("p (a b) -> p a b", b=2),
                in_=attn_b[:, :, 0:2]), dbg_s, 16)
        if tw >= P:
            E("sync", lambda e, t0=t0, tw=tw: e.dma_start(
                out=x1_dram[t0:t0 + tw, :].rearrange("(b p) f -> p b f", p=P),
                in_=x1t[:]), sp_x1, 16)
            E("sync", lambda e, t0=t0, tw=tw: e.dma_start(
                out=k1_dram[t0:t0 + tw, :].rearrange("(b p) f -> p b f", p=P),
                in_=k1b[:]), sp_k1, 16)
        else:
            E("sync", lambda e, t0=t0, tw=tw: e.dma_start(
                out=x1_dram[t0:t0 + tw, :], in_=x1t[0:tw, 0, :]), sp_x1, 16)
            E("sync", lambda e, t0=t0, tw=tw: e.dma_start(
                out=k1_dram[t0:t0 + tw, :], in_=k1b[0:tw, 0, :]), sp_k1, 16)
        x1t_spilled[0] = sp_x1.n
        if it < 8:
            pass
    SP_X1 = sp_x1.n

    W("vector", sp_k1, sp_k1.n)
    v_zk = E("vector", lambda e: e.memset(k1b[:], 0.0), vS)
    W("sync", vS, v_zk)
    E("sync", lambda e: e.dma_start(
        out=k1_dram[4098:4226, :].rearrange("(b p) f -> p b f", p=P),
        in_=k1b[:, 0:1, :]), sp_k1, 16)
    E("sync", lambda e: e.dma_start(
        out=k1_dram[4226:4352, :], in_=k1b[0:126, 1, :]), sp_k1, 16)
    SP_K1 = sp_k1.n
    MASKED = vS.n

    if DBG[0]:
        W("sync", vS, vS.n)
        E("sync", lambda e: e.dma_start(out=dbg_d[0:1, 0:4098], in_=r_row[0:1, 0:4098]), st_o, 16)
        E("gpsimd", lambda e: e.dma_start(out=dbg_d[1:2, 0:4098], in_=nm_row[0:1, 0:4098]), dbg_s, 16)
        E("sync", lambda e: e.dma_start(out=dbg_d[0:P, 4104:4138], in_=s1i_col[:, :]), st_o, 16)
        E("gpsimd", lambda e: e.dma_start(out=dbg_d[2:3, 0:3072], in_=wsrow_b[0:1, :]), dbg_s, 16)
        E("sync", lambda e: e.dma_start(out=dbg_d[3:4, 0:32], in_=crow[0:1, :]), st_o, 16)
        E("sync", lambda e: e.dma_start(out=dbg_d[:, 4160:4680], in_=kvz_sb[:]), st_o, 16)
        E("sync", lambda e: e.dma_start(out=dbg_d[:, 4680:4936], in_=absb[:]), st_o, 16)
        E("sync", lambda e: e.dma_start(out=dbg_d[:, 4712:4714], in_=swc[:]), st_o, 16)
    psB.close()
    B1_T = tS.n
    B1_V = vS.n
    B1_S = sS.n
    B1_G = gS.n
    b1.close()
    ab1.close()

    b2 = ExitStack()
    w1w = sball(b2, "w1w", [P, NCH, DH], BF16)
    w2w = sball(b2, "w2w", [P, HCH, D], BF16)
    s2p_sb = sball(b2, "s2p_sb", [P, 256], F32)
    k1f = sball(b2, "k1f", [P, 2, NCH, 256], BF16)
    hd_s = sball(b2, "hd_s", [P, 3, 256], F32)
    cu_s = sball(b2, "cu_s", [P, 2, 256], F32)
    cv_s = sball(b2, "cv_s", [P, 2, 256], F32)
    h2 = sball(b2, "h2", [P, HCH, 256], F32)
    habs = sball(b2, "habs", [P, 256], F32)
    hmin = sball(b2, "hmin", [P, 256], F32)
    k2i = sball(b2, "k2i", [P, 256], I32)
    k2b = sball(b2, "k2b", [P, 2, 256], BF16)
    x1l = sball(b2, "x1l", [P, 2, 2, D], F32)
    oscl = sball(b2, "oscl", [P, D], F32)
    s1sb = sball(b2, "s1sb", [P, 256], F32)

    psC = ExitStack()
    bankA = psC.enter_context(nc.psum_tensor("bankA", [P, 512], F32))
    bankB = psC.enter_context(nc.psum_tensor("bankB", [P, 512], F32))
    bankC = psC.enter_context(nc.psum_tensor("bankC", [P, 512], F32))
    bankD = psC.enter_context(nc.psum_tensor("bankD", [P, 512], F32))
    o2p = [[psC.enter_context(nc.psum_tensor("o2p%d%d" % (tb, h), [P, 512], F32))
            for h in range(2)] for tb in range(2)]
    hps = [bankA[:, 0:256], bankB[:, 0:256]]
    mtp = bankA[:, 256:512]
    s2p = bankB[:, 256:512]
    s1ip = bankC[:, 0:256]
    s2rw = bankC[0:1, 256:512]
    s1rw = bankD[0:1, 0:384]

    def fence2(eng):
        W(eng, tS, B1_T)
        W(eng, vS, B1_V)
        W(eng, sS, B1_S)
        W(eng, gS, B1_G)

    fence2("gpsimd")
    W("gpsimd", sp_wq, SP_WQ_ALL)
    LDW_1 = E("gpsimd", lambda e: e.dma_start(
        out=w1w[:], in_=w1q_dram.rearrange("(c p) n -> p c n", p=P)), ld_w, 16)
    LDW_2 = E("gpsimd", lambda e: e.dma_start(
        out=w2w[:], in_=w2q_dram.rearrange("(c p) n -> p c n", p=P)), ld_w, 16)

    k1f_free = [0, 0]
    x1_free = [0, 0]
    oscl_free = [0]
    hd_free = [0, 0]
    cu_free = [0]
    cv_free = [0, 0]
    mt_free = [0]
    mt_t_prev = [0]
    s2p_free = [0]
    s1ip_free = [0]
    s2rw_free = [0]
    s1rw_free = [0]
    k2b_free = [0, 0]
    k2i_free = [0]
    o2_free = [[0, 0], [0, 0]]
    st_done = [0, 0]

    fence2("sync")
    fence2("scalar")

    E("vector", lambda e: e.memset(habs[:], 0.0), vS)
    E("vector", lambda e: e.memset(hmin[:], 0.0), vS)

    for j in range(NWIN):
        par = j % 2
        r0 = j * WSTRIDE
        wout = min(WSTRIDE, T - r0)
        wH = wout + 2
        nob = (wout + P - 1) // P

        if j >= 2:
            W("sync", tS, k1f_free[par])
            W("scalar", tS, k1f_free[par])
        if j == 0:
            W("sync", sp_k1, SP_K1)
            W("scalar", sp_k1, SP_K1)
        for c in range(NCH):
            for tb in range(2):
                E("sync", lambda e, c=c, tb=tb, r0=r0, par=par: e.dma_start_transpose(
                    k1f[:, par, c, tb * P:(tb + 1) * P],
                    k1_dram[r0 + tb * P: r0 + (tb + 1) * P, c * P:(c + 1) * P]),
                    ld_k1, 16)
        LD_K = ld_k1.n

        if j >= 2:
            W("sync", vS, x1_free[par])
            W("sync", st_o, st_done[par])
        if j == 0:
            W("sync", sp_x1, SP_X1)
        for tb in range(nob):
            bw = min(P, wout - tb * P)
            E("sync", lambda e, tb=tb, bw=bw, r0=r0, par=par: e.dma_start(
                out=x1l[0:bw, par, tb, :],
                in_=x1_dram[r0 + 1 + tb * P: r0 + 1 + tb * P + bw, :]), ld_x1, 16)
        LD_X = ld_x1.n

        m0 = r0 // P
        off = r0 - m0 * P
        if j == 0:
            W("tensor", vS, MASKED)
            W("tensor", ld_w, LDW_1)
        W("tensor", vS, s1rw_free[0])
        t_tr = None
        for mi in range(3):
            m = min(m0 + mi, 33)
            t_tr = E("tensor", lambda e, mi=mi, m=m: e.transpose(
                s1rw[0:1, mi * P:(mi + 1) * P], s1i_col[:, m:m + 1], ident_f[:]), tS)
        W("vector", tS, t_tr)
        v_s1row = E("vector", lambda e, off=off, wH=wH: e.tensor_scalar(
            s1r[0:1, 3 * P:3 * P + wH], s1rw[0:1, off:off + wH],
            crow[0:1, 0:1], None, ALU.mult), vS)
        s1rw_free[0] = v_s1row
        if j == 0:
            SPC4()
            v_s1row = E("vector", lambda e: e.tensor_mul(
                s1r[0:1, 3 * P:3 * P + 1], s1r[0:1, 3 * P:3 * P + 1],
                msk_t[0:1, 0:1]), vS)
        if j == NWIN - 1:
            SPC4()
            pos = 3 * P + wH - 1
            v_s1row = E("vector", lambda e, pos=pos: e.tensor_mul(
                s1r[0:1, pos:pos + 1], s1r[0:1, pos:pos + 1],
                msk_t[0:1, 1:2]), vS)
        W("tensor", vS, v_s1row)
        W("tensor", sS, s1ip_free[0])
        t_s1i = E("tensor", lambda e, wH=wH: e.matmul(
            s1ip[:, 0:wH], ones_row_f[:], s1r[0:1, 3 * P:3 * P + wH],
            start=True, stop=True), tS)
        W("scalar", tS, t_s1i)
        s_s1sb = E("scalar", lambda e, wH=wH: e.copy(s1sb[:, 0:wH], s1ip[:, 0:wH]), sS)
        s1ip_free[0] = s_s1sb
        if DBG[0] and j == 9:
            W("sync", sS, s_s1sb)
            E("sync", lambda e: e.dma_start(out=dbg_d[:, 22400:22656], in_=s1sb[:, 0:256]), st_o, 16)
            W("gpsimd", ld_k1, LD_K)
            E("gpsimd", lambda e, par=par: e.dma_start(
                out=dbg_d[:, 10400:12448].rearrange("p (a b) -> p a b", b=256),
                in_=k1f[:, par, :, :]), dbg_s, 16)

        W("tensor", ld_k1, LD_K)
        W("tensor", vS, max(mt_free[0], s2p_free[0]))
        g_habs = 0
        si_v = {}
        for k in range(HCH + 3):
            if k < HCH:
                hpar = k % 2
                W("tensor", vS, hd_free[hpar])
                t_h = None
                for c in range(NCH):
                    t_h = E("tensor", lambda e, k=k, c=c, par=par, hpar=hpar, wH=wH: e.matmul(
                        hps[hpar][:, 0:wH], w1w[:, c, k * P:(k + 1) * P],
                        k1f[:, par, c, 0:wH], start=(c == 0), stop=(c == NCH - 1)),
                        tS if c == NCH - 1 else None)
                if k == HCH - 1:
                    k1f_free[par] = tS.n
                W("vector", tS, t_h)
                if k == 0:
                    W("vector", sS, s_s1sb)
                v_dq = E("vector", lambda e, k=k, hpar=hpar, wH=wH: e.tensor_mul(
                    hd_s[:, k % 3, 0:wH], hps[hpar][:, 0:wH], s1sb[:, 0:wH]), vS)
                hd_free[hpar] = v_dq
            kb = k - 1
            if 0 <= kb < HCH:
                E("vector", lambda e, kb=kb, wout=wout: e.scalar_tensor_tensor(
                    cu_s[:, kb % 2, 0:wout], hd_s[:, kb % 3, 0:wout], cwa[:, kb:kb + 1],
                    hd_s[:, kb % 3, 1:1 + wout], ALU.mult, ALU.add), vS)
            kc = k - 2
            if 0 <= kc < HCH:
                W("vector", sS, cv_free[kc % 2])
                g_cv = E("vector", lambda e, kc=kc, wout=wout: e.scalar_tensor_tensor(
                    cv_s[:, kc % 2, 0:wout], hd_s[:, kc % 3, 2:2 + wout], cwb[:, kc:kc + 1],
                    cu_s[:, kc % 2, 0:wout], ALU.mult, ALU.add), vS)
                W("scalar", vS, g_cv)
                si_v[kc] = E("scalar", lambda e, kc=kc, wout=wout: e.activation(
                    out=h2[:, kc, 0:wout], in_=cv_s[:, kc % 2, 0:wout], func=AF.Silu,
                    bias=cbt[:, kc:kc + 1], scale=cwt[:, kc, 1:2]), sS)
                cv_free[kc % 2] = si_v[kc]
                if DBG[0] and j in (0, 9) and kc == 7:
                    W("sync", sS, si_v[7])
                    base = 6200 if j == 9 else 8300
                    E("sync", lambda e, base=base: e.dma_start(
                        out=dbg_d[:, base:base + 2032].rearrange("p (a b) -> p a b", b=254),
                        in_=h2[:, 0:8, 0:254]), st_o, 16)
            kd = k - 3
            if 0 <= kd < HCH:
                W("vector", sS, si_v[kd])
                if kd == 0:
                    W("vector", tS, mt_t_prev[0])
                    E("vector", lambda e, wout=wout: e.tensor_copy(
                        habs[:, 0:wout], h2[:, 0, 0:wout]), vS)
                    g_habs = E("vector", lambda e, wout=wout: e.tensor_copy(
                        hmin[:, 0:wout], h2[:, 0, 0:wout]), vS)
                else:
                    E("vector", lambda e, kd=kd, wout=wout: e.tensor_tensor(
                        habs[:, 0:wout], habs[:, 0:wout], h2[:, kd, 0:wout], ALU.max), vS)
                    g_habs = E("vector", lambda e, kd=kd, wout=wout: e.tensor_tensor(
                        hmin[:, 0:wout], hmin[:, 0:wout], h2[:, kd, 0:wout], ALU.min), vS)

        W("vector", vS, g_habs)
        E("vector", lambda e: e.tensor_scalar(hmin[:], hmin[:], -1.0, None, ALU.mult), vS)
        v_habs = E("vector", lambda e: e.tensor_tensor(habs[:], habs[:], hmin[:], ALU.max), vS)
        W("tensor", vS, v_habs)
        t_mt = None
        for tb in range(2):
            t_mt = E("tensor", lambda e, tb=tb: e.transpose(
                mtp[:, tb * P:(tb + 1) * P], habs[:, tb * P:(tb + 1) * P], ident_f[:]), tS)
        mt_t_prev[0] = t_mt
        W("vector", tS, t_mt)
        for tb in range(2):
            E("vector", lambda e, tb=tb: e.tensor_reduce(
                s2cc[:, 8 * tb:8 * tb + 1], mtp[:, tb * P:(tb + 1) * P], AX.X, ALU.max), vS)
        mt_free[0] = vS.n
        if j == 0:
            W("vector", ld_w, LDW_2)
        s2v = s2cc[:].rearrange("p (a b) -> p a b", b=8)
        SPC4()
        E("vector", lambda e, s2v=s2v: e.tensor_scalar(
            s2v[:, 2:4, 0:1], s2v[:, 0:2, 0:1], eps15_col[:], mw2c127[:], ALU.max, ALU.mult), vS)
        v_s2c = E("vector", lambda e, s2v=s2v: e.tensor_scalar(
            s2v[:, 4:6, 0:1], s2v[:, 0:2, 0:1], eps15_col[:], c127_col[:], ALU.max, ALU.mult), vS)
        W("tensor", vS, max(v_s2c, s2rw_free[0]))
        t_s2t = None
        for tb in range(2):
            t_s2t = E("tensor", lambda e, tb=tb: e.transpose(
                s2rw[0:1, tb * P:(tb + 1) * P], s2cc[:, 8 * (4 + tb):8 * (4 + tb) + 1], ident_f[:]), tS)
        W("vector", tS, t_s2t)
        v_s2r = E("vector", lambda e: e.tensor_copy(s2r[0:1, 0:256], s2rw[0:1, 0:256]), vS)
        s2rw_free[0] = v_s2r
        W("tensor", vS, v_s2r)
        t_s2p = E("tensor", lambda e: e.matmul(
            s2p[:, 0:256], ones_row_f[:], s2r[0:1, 0:256], start=True, stop=True), tS)
        if j == 0:
            W("tensor", ld_w, LDW_2)

        W("vector", tS, t_s2p)
        v_s2w = E("vector", lambda e: e.reciprocal(s2p_sb[:, 0:256], s2p[:, 0:256]), vS)
        s2p_free[0] = v_s2w
        v_k2_last = 0
        for oc in range(HCH):
            kpar = oc % 2
            W("vector", tS, k2b_free[kpar])
            W("vector", sS, k2i_free[0])
            v_k2i = E("vector", lambda e, oc=oc, wout=wout: e.tensor_tensor(
                k2i[:, 0:wout], h2[:, oc, 0:wout], s2p_sb[:, 0:wout], ALU.mult), vS)
            v_k2_last = v_k2i
            W("scalar", vS, v_k2i)
            s_k2b = E("scalar", lambda e, oc=oc, kpar=kpar, wout=wout: e.copy(
                k2b[:, kpar, 0:wout], k2i[:, 0:wout]), sS)
            k2i_free[0] = s_k2b
            W("tensor", sS, s_k2b)
            if oc == 0:
                for tb in range(nob):
                    for h in range(2):
                        W("tensor", sS, o2_free[tb][h])
            t_last = None
            for tb in range(nob):
                bw = min(P, wout - tb * P)
                for h in range(2):
                    t_last = E("tensor", lambda e, oc=oc, tb=tb, h=h, bw=bw, kpar=kpar: e.matmul(
                        o2p[tb][h][0:bw, 0:512],
                        k2b[:, kpar, tb * P:tb * P + bw],
                        w2w[:, oc, h * 512:(h + 1) * 512],
                        start=(oc == 0), stop=(oc == HCH - 1)),
                        tS if (tb == nob - 1 and h == 1) else None)
            k2b_free[kpar] = t_last
        for kp in range(2):
            if k2b_free[kp] is None:
                k2b_free[kp] = tS.n
        T_O2 = tS.n

        if DBG[0]:
            W("sync", vS, v_s2w)
            E("sync", lambda e, j=j: e.dma_start(
                out=dbg_d[:, 18000 + 256 * j:18256 + 256 * j], in_=habs[:]), st_o, 16)
            E("sync", lambda e, j=j: e.dma_start(
                out=dbg_d[:, 16000 + 64 * j:16048 + 64 * j], in_=s2cc[:]), st_o, 16)
        if DBG[0] and j == 2:
            E("sync", lambda e: e.dma_start(out=dbg_d[0:1, 24470:24726],
                                            in_=s1r[0:1, 3 * P:3 * P + 256]), st_o, 16)
        W("scalar", tS, T_O2)
        for tb in range(nob):
            bw = min(P, wout - tb * P)
            for h in range(2):
                W("scalar", vS, oscl_free[0])
                s_os = E("scalar", lambda e, tb=tb, h=h, bw=bw: e.activation(
                    out=oscl[0:bw, h * 512:(h + 1) * 512], in_=o2p[tb][h][0:bw, 0:512],
                    func=AF.Copy, bias=0.0, scale=s2cc[0:bw, 16 + 8 * tb:17 + 8 * tb]), sS)
                o2_free[tb][h] = s_os
                W("vector", sS, s_os)
                W("vector", ld_x1, LD_X)
                if j >= 2 and tb == 0 and h == 0:
                    W("vector", st_o, st_done[par])
                oscl_free[0] = E("vector", lambda e, tb=tb, h=h, bw=bw, par=par: e.tensor_add(
                    x1l[0:bw, par, tb, h * 512:(h + 1) * 512],
                    oscl[0:bw, h * 512:(h + 1) * 512],
                    x1l[0:bw, par, tb, h * 512:(h + 1) * 512]), vS)
            v_fin = vS.n
            W("sync", vS, v_fin)
            E("sync", lambda e, tb=tb, bw=bw, r0=r0, par=par: e.dma_start(
                out=out_d[r0 + tb * P: r0 + tb * P + bw, :],
                in_=x1l[0:bw, par, tb, :]), st_o, 16)
        x1_free[par] = vS.n
        st_done[par] = st_o.n

    W("sync", st_o, st_o.n)
    if DBG[0]:
        W("sync", dbg_s, dbg_s.n)
    W("sync", sp_x1, SP_X1)
    W("sync", sp_k1, SP_K1)
    W("sync", sp_wq, SP_WQ_ALL)

    psC.close()
    b2.close()
    semstk.close()
    es.close()

    with nc.Block() as block:
        @block.sync
        def _(e):
            for f in prog["sync"]:
                f(e)

        @block.scalar
        def _(e):
            for f in prog["scalar"]:
                f(e)

        @block.tensor
        def _(e):
            for f in prog["tensor"]:
                f(e)

        @block.vector
        def _(e):
            for f in prog["vector"]:
                f(e)

        @block.gpsimd
        def _(e):
            for f in prog["gpsimd"]:
                f(e)

    return nc


_NC_CACHE = [None]


def kernel(L0_attn_x, L0_attn_Wq, L0_attn_bq, L0_attn_Wk, L0_attn_bk,
           L0_attn_Wv, L0_attn_bv, L0_attn_Wo, L0_attn_bo,
           L0_ffn_w1, L0_ffn_conv_w, L0_ffn_conv_b, L0_ffn_w2):
    x = np.ascontiguousarray(np.asarray(L0_attn_x, dtype=np.float32))
    B, N, _ = x.shape
    assert (B, N) == (2, 16384)

    def cont(a):
        return np.ascontiguousarray(np.asarray(a, dtype=np.float32))

    wqT = np.ascontiguousarray(cont(L0_attn_Wq).T)
    wkT = np.ascontiguousarray(cont(L0_attn_Wk).T)
    wvT = np.ascontiguousarray(cont(L0_attn_Wv).T)
    woT = np.ascontiguousarray(cont(L0_attn_Wo).T)
    w1T = np.ascontiguousarray(cont(L0_ffn_w1).T)
    w2T = np.ascontiguousarray(cont(L0_ffn_w2).T)
    bq = cont(L0_attn_bq).reshape(1, D)
    bk = cont(L0_attn_bk).reshape(1, D)
    bv = cont(L0_attn_bv).reshape(1, D)
    bo = cont(L0_attn_bo).reshape(1, D)
    cw = np.ascontiguousarray(cont(L0_ffn_conv_w).reshape(HCH, P, 3).transpose(1, 0, 2))
    cb = np.ascontiguousarray(cont(L0_ffn_conv_b).reshape(HCH, P).T)

    if _NC_CACHE[0] is None:
        _NC_CACHE[0] = build_program()
    nc = _NC_CACHE[0]

    in_maps = []
    for core in range(8):
        b = core // 4
        s = (core % 4) * T
        lo = max(s - 1, 0)
        hi = min(s + T + 1, N)
        sl = x[b, lo:hi, :]
        if s - 1 < 0:
            sl = np.concatenate([sl[0:1], sl], axis=0)
        if s + T + 1 > N:
            sl = np.concatenate([sl, sl[-1:]], axis=0)
        xN_c = np.ascontiguousarray(sl)
        xT_c = np.ascontiguousarray(sl.T)
        mL = 0.0 if s == 0 else 1.0
        mR = 0.0 if s + T == N else 1.0
        msk = np.broadcast_to(np.array([mL, mR], np.float32), (P, 2)).copy()
        in_maps.append({
            "xT": xT_c, "xN": xN_c,
            "wqT": wqT, "wkT": wkT, "wvT": wvT, "woT": woT,
            "w1T": w1T, "w2T": w2T,
            "bq": bq, "bk": bk, "bv": bv, "bo": bo,
            "cw": cw, "cb": cb, "msk": msk,
        })

    res = run_bass_kernel_spmd(nc, in_maps, core_ids=list(range(8)))
    if DBG[0]:
        kernel.dbg_results = res.results
    outp = np.empty((2, 16384, D), dtype=np.float32)
    for core in range(8):
        b = core // 4
        s = (core % 4) * T
        outp[b, s:s + T, :] = res.results[core]["out"]
    return outp
